# version 03
# baseline (speedup 1.0000x reference)
import math
from contextlib import ExitStack

import numpy as np
import ml_dtypes
import concourse.bass as bass
import concourse.mybir as mybir
from concourse.bass_utils import run_bass_kernel_spmd

F32 = mybir.dt.float32
BF16 = mybir.dt.bfloat16
AF = mybir.ActivationFunctionType
ALU = mybir.AluOpType
AX = mybir.AxisListType

NCORES = 8
T = 2048
D = 1024
NT = T // 512
ND = D // 128

ENGINES = ("sync", "scalar", "vector", "tensor", "gpsimd")
RACE_DETECT = True


class Sem:
    __slots__ = ("h", "n", "name")


class Prog:

    def __init__(self, nc, es):
        self.nc = nc
        self.es = es
        self.ops = {e: [] for e in ENGINES}
        self._sem_ctr = 0
        self.paranoid_sems = []

    def sem(self, name=None):
        s = Sem()
        self._sem_ctr += 1
        s.name = name or f"s{self._sem_ctr}"
        s.h = self.es.enter_context(self.nc.semaphore(s.name))
        s.n = 0
        return s

    def op(self, eng, fn, waits=(), inc=None, inc_k=1):
        if inc is not None:
            inc.n += inc_k
        if self.paranoid_sems and eng in ("vector", "scalar", "tensor"):
            waits = list(waits) + [(s, s.n - (1 if s is inc else 0))
                                   for s in self.paranoid_sems]
        wl = tuple((s, v) for (s, v) in waits if v > 0)

        def emit(e, dedup, fn=fn, wl=wl, inc=inc, inc_k=inc_k):
            for (s, v) in wl:
                if dedup.get(s.name, -1) < v:
                    e.wait_ge(s.h, v)
                    dedup[s.name] = v
            r = fn(e)
            if inc is not None:
                r.then_inc(inc.h, inc_k)

        self.ops[eng].append(emit)

    def raw(self, eng, fn):
        self.ops[eng].append(fn)

    def fence(self, eng, sem):
        v = sem.n

        def emit(e, dedup, v=v, sem=sem):
            if dedup.get(sem.name, -1) < v:
                e.wait_ge(sem.h, v)
                dedup[sem.name] = v

        self.ops[eng].append(emit)

    def finalize(self, block):
        for eng in ENGINES:
            lst = self.ops[eng]
            if not lst:
                continue

            def runner(e, lst=lst):
                dedup = {}
                for f in lst:
                    f(e, dedup)

            getattr(block, eng)(runner)


def ceil_div(a, b):
    return (a + b - 1) // b



def transpose_tiles(P, pairs, ident, psums, s_tp, s_ev, evac_eng="vector",
                    extra_waits=(), evac_fn=None):
    npp = len(psums)
    for i, (src, dst) in enumerate(pairs):
        pb = psums[i % npp]
        w = [(s_ev, s_ev.n - (npp - 1))] + list(extra_waits)
        P.op("tensor", lambda e, src=src, pb=pb: e.transpose(pb[:], src, ident[:]),
             waits=w, inc=s_tp)
        if evac_fn is None:
            P.op(evac_eng, lambda e, pb=pb, dst=dst: e.tensor_copy(dst, pb[:]),
                 waits=[(s_tp, s_tp.n)], inc=s_ev)
        else:
            P.op(evac_eng, lambda e, pb=pb, dst=dst: evac_fn(e, pb[:], dst),
                 waits=[(s_tp, s_tp.n)], inc=s_ev)


def load_x_to_fm(P, x_hbm, xb_tiles, ident_bf, psums_bf, s_ld, s_tp, s_fm, tag="",
                 extra=()):
    nc = P.nc
    stage = P.es.enter_context(nc.sbuf_tensor(f"ldstage{tag}", [128, D], BF16))
    xr = x_hbm.rearrange("(n p) d -> n p d", p=128)
    for tt in range(T // 128):
        P.op("gpsimd", lambda e, tt=tt, stage=stage: e.dma_start(out=stage[:], in_=xr[tt]),
             waits=[(s_tp, s_tp.n)], inc=s_ld, inc_k=16)
        pairs = [(stage[:, ck * 128:(ck + 1) * 128],
                  xb_tiles[ck][:, tt * 128:(tt + 1) * 128]) for ck in range(ND)]
        transpose_tiles(P, pairs, ident_bf, psums_bf, s_tp, s_fm,
                        extra_waits=[(s_ld, s_ld.n)] + list(extra))


def store_fm_f32_to_hbm(P, src_tiles_f32, out_hbm, ident_f32, psums_f32,
                        s_tp, s_ev, s_st, src_ready):
    nc = P.nc
    stages = [P.es.enter_context(nc.sbuf_tensor(f"ststage{i}", [128, D], F32))
              for i in range(2)]
    orr = out_hbm.rearrange("(n p) d -> n p d", p=128)
    for tt in range(T // 128):
        stage = stages[tt % 2]
        pairs = [(src_tiles_f32[ck][:, tt * 128:(tt + 1) * 128],
                  stage[:, ck * 128:(ck + 1) * 128]) for ck in range(ND)]
        transpose_tiles(P, pairs, ident_f32, psums_f32, s_tp, s_ev,
                        extra_waits=list(src_ready) + [(s_st, 16 * (tt - 1))])
        P.op("sync", lambda e, tt=tt, stage=stage: e.dma_start(out=orr[tt], in_=stage[:]),
             waits=[(s_ev, s_ev.n)], inc=s_st, inc_k=16)


def load_vec_cols(P, vec_hbm, n, dst_cols, ident_f32, psum_f32, s_ld, s_tp, s_ev,
                  dma_waits=()):
    k = n // 128
    nc = P.nc
    if not hasattr(P, "_vcc"):
        P._vcc = 0
    P._vcc += 1
    stage = P.es.enter_context(nc.sbuf_tensor(f"veccol{P._vcc}", [max(k, 1), 128], F32))
    vr = vec_hbm.rearrange("(a p) -> a p", p=128)
    P.op("sync", lambda e: e.dma_start(out=stage[0:k, :], in_=vr),
         waits=list(dma_waits), inc=s_ld, inc_k=16)
    P.op("tensor", lambda e: e.transpose(psum_f32[0:128, 0:k], stage[0:k, :], ident_f32[0:k, 0:k]),
         waits=[(s_ld, s_ld.n), (s_ev, s_ev.n)], inc=s_tp)
    P.op("vector", lambda e: e.tensor_copy(dst_cols, psum_f32[0:128, 0:k]),
         waits=[(s_tp, s_tp.n)], inc=s_ev)


def emit_ln(P, B, tc, rs_ready, out_bf_tiles, out_f32_tiles=None,
            gain_col=None, bias_col=None, eps=1e-5):
    s_t, s_v, s_s = B.s_t, B.s_v, B.s_s
    pmu, psq, pmuB, prstdB = B.pm[2], B.pm[3], B.pm[4], B.pm[5]
    for ct in range(ND):
        P.op("scalar", lambda e, ct=ct: e.activation(B.SQ[ct][:], B.RS[ct][:], AF.Square),
             waits=rs_ready, inc=s_s)
    sq_done = s_s.n
    for ct in range(ND):
        P.op("tensor", lambda e, ct=ct: P.nc.tensor.matmul(
            pmu[0:1, :], B.ones_col_f32, B.RS[ct][:], start=(ct == 0), stop=(ct == ND - 1)),
             waits=list(rs_ready) + [(B.s_ln_ev, B.s_ln_ev.n)], inc=s_t)
    for ct in range(ND):
        P.op("tensor", lambda e, ct=ct: P.nc.tensor.matmul(
            psq[0:1, :], B.ones_col_f32, B.SQ[ct][:], start=(ct == 0), stop=(ct == ND - 1)),
             waits=[(s_s, sq_done)], inc=s_t)
    mm_done = s_t.n
    mu_r, ex2_r, rstd_r = B.row4
    std_r = ex2_r
    P.op("vector", lambda e: e.tensor_scalar_mul(mu_r, pmu[0:1, :], 1.0 / D),
         waits=[(s_t, mm_done)], inc=s_v)
    P.op("vector", lambda e: e.tensor_scalar_mul(ex2_r, psq[0:1, :], 1.0 / D),
         inc=s_v)
    P.op("vector", lambda e: e.tensor_mul(rstd_r, mu_r, mu_r), inc=s_v)
    P.op("vector", lambda e: e.tensor_sub(ex2_r, ex2_r, rstd_r), inc=s_v)
    v_done = s_v.n
    P.op("scalar", lambda e: e.activation(std_r, ex2_r, AF.Sqrt, bias=B.eps5_col[0:1, :]),
         waits=[(s_v, v_done)], inc=s_s)
    P.op("vector", lambda e: e.reciprocal(rstd_r, std_r), waits=[(s_s, s_s.n)], inc=s_v)
    rows_done = s_v.n
    P.op("tensor", lambda e: P.nc.tensor.matmul(
        pmuB[:], B.ones_row_f32, mu_r, start=True, stop=True),
         waits=[(s_v, v_done), (B.s_ln_ev, B.s_ln_ev.n)], inc=s_t)
    P.op("tensor", lambda e: P.nc.tensor.matmul(
        prstdB[:], B.ones_row_f32, rstd_r, start=True, stop=True),
         waits=[(s_v, rows_done), (B.s_ln_ev, B.s_ln_ev.n)], inc=s_t)
    bc_done = s_t.n
    for ct in range(ND):
        P.op("vector", lambda e, ct=ct: e.tensor_sub(B.SQ[ct][:], B.RS[ct][:], pmuB[:]),
             waits=[(s_t, bc_done)], inc=s_v)
        if gain_col is not None:
            P.op("vector", lambda e, ct=ct: e.tensor_mul(B.SQ[ct][:], B.SQ[ct][:], prstdB[:]),
                 inc=s_v)
            P.op("vector", lambda e, ct=ct: e.tensor_scalar(
                out_bf_tiles[ct][:, tc * 512:(tc + 1) * 512], B.SQ[ct][:],
                gain_col[ct], bias_col[ct], ALU.mult, ALU.add), inc=B.s_ln_ev)
        elif out_bf_tiles is not None:
            P.op("vector", lambda e, ct=ct: e.tensor_mul(
                out_bf_tiles[ct][:, tc * 512:(tc + 1) * 512], B.SQ[ct][:], prstdB[:]),
                 inc=B.s_ln_ev)
        if out_f32_tiles == "rs":
            P.op("vector", lambda e, ct=ct: e.tensor_mul(
                B.RS[ct][:], B.SQ[ct][:], prstdB[:]), inc=B.s_ln_ev)
        elif out_f32_tiles is not None:
            P.op("vector", lambda e, ct=ct: e.tensor_mul(
                out_f32_tiles[ct][:, tc * 512:(tc + 1) * 512], B.SQ[ct][:], prstdB[:]),
                 inc=B.s_ln_ev)



class NS:
    pass


CW = 264 + 1024

def make_const_f32():
    c = np.zeros((128, CW), np.float32)
    c[:, 0:128] = np.eye(128, dtype=np.float32)
    c[:, 128] = 1.0
    c[0, 129:257] = 1.0
    c[:, 257] = -12.0
    c[:, 258] = 1e-5
    c[:, 259] = 1e-12
    c[:, 260] = 1e-6
    c[:, 261] = 1.0
    for ei in range(8):
        c[ei, 264 + ei * 128: 264 + (ei + 1) * 128] = 1.0
    return c


CWB = 129 + 384

def make_const_bf16():
    c = np.zeros((128, CWB), np.float32)
    c[:, 0:128] = np.eye(128, dtype=np.float32)
    c[:, 128] = 1.0
    iu = np.triu(np.ones((128, 128), np.float32), 1)
    il = np.tril(np.ones((128, 128), np.float32), -1)
    tu = np.triu(np.ones((128, 128), np.float32), 0)
    c[:, 129:257] = -iu
    c[:, 257:385] = -il
    c[:, 385:513] = tu
    return c.astype(ml_dtypes.bfloat16)


def alloc_common(P, nc):
    es = P.es
    B = NS()
    B.XB = [es.enter_context(nc.sbuf_tensor(f"XB{i}", [128, T], BF16)) for i in range(ND)]
    B.ACC = [es.enter_context(nc.sbuf_tensor(f"ACC{i}", [128, T], BF16)) for i in range(ND)]
    B.RS = [es.enter_context(nc.sbuf_tensor(f"RS{i}", [128, 512], F32)) for i in range(ND)]
    B.SQ = [es.enter_context(nc.sbuf_tensor(f"SQ{i}", [128, 512], F32)) for i in range(ND)]
    B.rows = es.enter_context(nc.sbuf_tensor("rows", [1, 3 * 512], F32))
    B.row4 = [B.rows[0:1, i * 512:(i + 1) * 512] for i in range(3)]
    B.cols = es.enter_context(nc.sbuf_tensor("cols", [128, 64], F32))
    B.c_f32 = es.enter_context(nc.sbuf_tensor("c_f32", [128, CW], F32))
    B.c_bf16 = es.enter_context(nc.sbuf_tensor("c_bf16", [128, CWB], BF16))
    B.ident_f32 = B.c_f32[:, 0:128]
    B.ones_col_f32 = B.c_f32[:, 128:129]
    B.ones_row_f32 = B.c_f32[0:1, 129:257]
    B.ident_bf = B.c_bf16[:, 0:128]
    B.ones_col_bf = B.c_bf16[:, 128:129]
    B.neg_shift_col = B.c_f32[:, 257:258]
    B.eps5_col = B.c_f32[:, 258:259]
    B.eps12_col = B.c_f32[:, 259:260]
    B.eps6_col = B.c_f32[:, 260:261]
    B.sel8 = [B.c_f32[0:8, 264 + i * 128: 264 + (i + 1) * 128] for i in range(8)]
    B.mask_nsu = B.c_bf16[:, 129:257]
    B.mask_nsl = B.c_bf16[:, 257:385]
    B.mask_tiu = B.c_bf16[:, 385:513]
    B.iostg = es.enter_context(nc.sbuf_tensor("iostg", [128, D], F32))
    B.pm = [es.enter_context(nc.psum_tensor(f"pm{i}", [128, 512], F32)) for i in range(6)]
    B.pt = [es.enter_context(nc.psum_tensor(f"pt{i}", [128, 1024], BF16)) for i in range(2)]
    B.s_t = P.sem("t")
    B.s_v = P.sem("v")
    B.s_s = P.sem("s")
    B.s_g = P.sem("g")
    B.s_d = P.sem("d")
    B.s_tp = P.sem("tp")
    B.s_ev = P.sem("ev")
    B.s_ln_ev = P.sem("lnev")
    return B


def load_consts(P, B, cf32_hbm, cbf16_hbm):
    P.op("sync", lambda e: e.dma_start(out=B.c_f32[:], in_=cf32_hbm), inc=B.s_d, inc_k=16)
    P.op("sync", lambda e: e.dma_start(out=B.c_bf16[:], in_=cbf16_hbm), inc=B.s_d, inc_k=16)
    B.consts_ready = (B.s_d, B.s_d.n)



def emit_phaseC(P, B, W, prev_done, hp_list=None, do_out=True, dbg=None, tc_list=None):
    nc = P.nc
    s_t, s_v, s_s, s_g, s_d = B.s_t, B.s_v, B.s_s, B.s_g, B.s_d
    TAU = float(64 ** -0.25)
    SHIFT = 12.0

    B.WT = [P.es.enter_context(nc.sbuf_tensor(f"WTC{i}", [128, D], BF16)) for i in range(32)]
    es2 = ExitStack()
    wraw = [es2.enter_context(nc.sbuf_tensor(f"wrawC{i}", [128, D], BF16)) for i in range(2)]
    wnames = ("Wq", "Wk", "Wv", "Wout")
    for wi, wn in enumerate(wnames):
        wr = W[wn].rearrange("(r p) c -> r p c", p=128)
        for r in range(8):
            stg = wraw[(wi * 8 + r) % 2]
            P.op("gpsimd", lambda e, wr=wr, r=r, stg=stg: e.dma_start(out=stg[:], in_=wr[r]),
                 waits=list(prev_done) + [(B.s_tp, B.s_tp.n - ND)], inc=s_g, inc_k=16)
            dst_tiles = B.WT[wi * 8: wi * 8 + 8]
            pairs = [(stg[:, ck * 128:(ck + 1) * 128],
                      dst_tiles[ck][:, r * 128:(r + 1) * 128]) for ck in range(ND)]
            transpose_tiles(P, pairs, B.ident_bf, [p[:, 0:128] for p in B.pt],
                            B.s_tp, B.s_ev, extra_waits=[(s_g, s_g.n), B.consts_ready])
    P.fence("gpsimd", s_g)
    wts_done = (B.s_ev, B.s_ev.n)
    es2.close()
    load_vec_cols(P, W["bout"], D, B.cols[:, 0:8], B.ident_f32, B.pm[5], s_d, B.s_tp, B.s_ev,
                  dma_waits=[wts_done])
    bout_done = (B.s_ev, B.s_ev.n)

    qT = P.es.enter_context(nc.sbuf_tensor("qTC", [128, T], BF16))
    kT = P.es.enter_context(nc.sbuf_tensor("kTC", [128, T], BF16))
    vT = P.es.enter_context(nc.sbuf_tensor("vTC", [128, T], BF16))
    vtm = [P.es.enter_context(nc.sbuf_tensor(f"vtmC{i}", [128, 130], BF16)) for i in range(T // 128)]
    PT = P.es.enter_context(nc.sbuf_tensor("PTC", [128, 8 * 512], BF16))
    aTt = P.es.enter_context(nc.sbuf_tensor("aTtC", [128, 512], BF16))

    if hp_list is None:
        hp_list = range(8)
    xb_ready = prev_done
    for tt in range(T // 128):
        P.op("vector", lambda e, tt=tt: e.memset(vtm[tt][:, 64:65], 1.0), inc=s_v)
        P.op("vector", lambda e, tt=tt: e.memset(vtm[tt][:, 129:130], 1.0), inc=s_v)

    for hp in hp_list:
        pair_prev = []
        for pi, (wbase, dstT, scale) in enumerate(((0, qT, TAU), (8, kT, TAU), (16, vT, None))):
            for tc in range(NT):
                pmx = B.pm[(pi * NT + tc) % 2]
                for ck in range(ND):
                    P.op("tensor", lambda e, wbase=wbase, ck=ck, hp=hp, tc=tc, pmx=pmx:
                         nc.tensor.matmul(pmx[:],
                                          B.WT[wbase + ck][:, hp * 128:(hp + 1) * 128],
                                          B.XB[ck][:, tc * 512:(tc + 1) * 512],
                                          start=(ck == 0), stop=(ck == ND - 1)),
                         waits=list(xb_ready) + [(wts_done[0], wts_done[1]), (s_s, s_s.n - 1)],
                         inc=s_t)
                if scale is not None:
                    P.op("scalar", lambda e, dstT=dstT, tc=tc, pmx=pmx, scale=scale:
                         e.mul(dstT[:, tc * 512:(tc + 1) * 512], pmx[:], scale),
                         waits=[(s_t, s_t.n)], inc=s_s)
                else:
                    P.op("scalar", lambda e, dstT=dstT, tc=tc, pmx=pmx:
                         e.copy(dstT[:, tc * 512:(tc + 1) * 512], pmx[:]),
                         waits=[(s_t, s_t.n)], inc=s_s)
        qkv_done = (s_s, s_s.n)
        if dbg is not None and hp == dbg.get("hp", 0):
            for nm, tile in (("qT", qT), ("kT", kT), ("vT", vT)):
                if nm in dbg:
                    P.op("sync", lambda e, nm=nm, tile=tile: e.dma_start(
                         out=dbg[nm], in_=tile[:]), waits=[qkv_done], inc=s_d, inc_k=16)

        for tt in range(T // 128):
            pb = B.pt[tt % 2]
            P.op("tensor", lambda e, tt=tt, pb=pb: e.transpose(
                 pb[:, 0:128], vT[:, tt * 128:(tt + 1) * 128], B.ident_bf[:]),
                 waits=[(qkv_done[0], qkv_done[1]), (B.s_ev, B.s_ev.n - 2)], inc=B.s_tp)
            P.op("vector", lambda e, tt=tt, pb=pb: e.tensor_copy(vtm[tt][:, 0:64], pb[:, 0:64]),
                 waits=[(B.s_tp, B.s_tp.n)], inc=B.s_ev)
            P.op("vector", lambda e, tt=tt, pb=pb: e.tensor_copy(vtm[tt][:, 65:129], pb[:, 64:128]),
                 inc=B.s_ev)
        vtm_done = (B.s_ev, B.s_ev.n)
        if dbg is not None and hp == dbg.get("hp", 0) and "vtm" in dbg:
            vr = dbg["vtm"].rearrange("a p b -> a p b")
            for tt in range(T // 128):
                P.op("sync", lambda e, tt=tt: e.dma_start(
                     out=dbg["vtm"][tt], in_=vtm[tt][:]), waits=[vtm_done], inc=s_d, inc_k=16)

        for hh in range(2):
            hs = slice(hh * 64, hh * 64 + 64)
            lh_lo = 0 if hh == 0 else 65
            for ic in range(NT):
                for wave in range(2):
                    av_prev = s_t.n
                    for j8 in range(8):
                        jt = wave * 8 + j8
                        pmx = B.pm[jt % 2]
                        P.op("tensor", lambda e, hs=hs, jt=jt, ic=ic, pmx=pmx:
                             nc.tensor.matmul(pmx[:], kT[hs, jt * 128:(jt + 1) * 128],
                                              qT[hs, ic * 512:(ic + 1) * 512],
                                              start=True, stop=True),
                             waits=[(qkv_done[0], qkv_done[1]), (s_s, s_s.n - 1)],
                             inc=s_t)
                        P.op("scalar", lambda e, j8=j8, pmx=pmx:
                             e.activation(PT[:, j8 * 512:(j8 + 1) * 512], pmx[:],
                                          AF.Exp, bias=B.neg_shift_col),
                             waits=[(s_t, s_t.n), (s_t, av_prev)], inc=s_s)
                    exp_done = (s_s, s_s.n)
                    for j8 in range(8):
                        jt = wave * 8 + j8
                        P.op("tensor", lambda e, j8=j8, jt=jt, lh_lo=lh_lo:
                             nc.tensor.matmul(B.pm[2][0:65, :],
                                              vtm[jt][:, lh_lo:lh_lo + 65],
                                              PT[:, j8 * 512:(j8 + 1) * 512],
                                              start=(jt == 0), stop=(jt == 15)),
                             waits=[(exp_done[0], exp_done[1]), (s_v, s_v.n - 1),
                                    (s_s, s_s.n)],
                             inc=s_t)
                av_done = (s_t, s_t.n)
                P.op("vector", lambda e: e.reciprocal(B.row4[2], B.pm[2][64:65, :]),
                     waits=[(av_done[0], av_done[1])], inc=s_v)
                P.op("tensor", lambda e: nc.tensor.matmul(
                     B.pm[3][0:64, :], B.ones_row_f32[0:1, 0:64], B.row4[2],
                     start=True, stop=True),
                     waits=[(s_v, s_v.n)], inc=s_t)
                bc_done = (s_t, s_t.n)
                P.op("scalar", lambda e: e.copy(aTt[0:64, :], B.pm[2][0:64, :]),
                     waits=[(av_done[0], av_done[1])], inc=s_s)
                P.op("vector", lambda e, hh=hh, hp=hp, ic=ic: e.tensor_mul(
                     B.ACC[hp][hh * 64:(hh + 1) * 64, ic * 512:(ic + 1) * 512],
                     aTt[0:64, :], B.pm[3][0:64, :]),
                     waits=[(bc_done[0], bc_done[1]), (s_s, s_s.n)], inc=s_v)
    acc_done = (s_v, s_v.n)
    if dbg is not None and "ACC" in dbg:
        for ck in range(ND):
            P.op("sync", lambda e, ck=ck: e.dma_start(
                 out=dbg["ACC"][ck], in_=B.ACC[ck][:]), waits=[acc_done], inc=s_d, inc_k=16)
    if not do_out:
        fin = s_d.n
        P.raw("sync", lambda e, dedup: e.wait_ge(s_d.h, fin))
        return [(s_d, s_d.n)]

    if tc_list is None:
        tc_list = range(NT)
    for tc in tc_list:
        for ct in range(ND):
            pmx = B.pm[ct % 2]
            for hk in range(ND):
                P.op("tensor", lambda e, ct=ct, hk=hk, tc=tc, pmx=pmx:
                     nc.tensor.matmul(pmx[:], B.WT[24 + hk][:, ct * 128:(ct + 1) * 128],
                                      B.ACC[hk][:, tc * 512:(tc + 1) * 512],
                                      start=(hk == 0), stop=(hk == ND - 1)),
                     waits=[acc_done, (s_v, s_v.n - 1), (s_s, s_s.n - 1), bout_done],
                     inc=s_t)
            if dbg is not None and dbg.get("PM") is not None and tc == 0 and ct == 0:
                allw = [(s_t, s_t.n), (s_v, s_v.n), (s_s, s_s.n), (s_g, s_g.n),
                        (s_d, s_d.n), (B.s_tp, B.s_tp.n), (B.s_ev, B.s_ev.n)]
                P.op("vector", lambda e, pmx=pmx: e.tensor_copy(B.SQ[7][:], pmx[:]),
                     waits=allw, inc=s_v)
                P.op("sync", lambda e: e.dma_start(out=dbg["PM"], in_=B.SQ[7][:]),
                     waits=[(s_v, s_v.n)], inc=s_d, inc_k=16)
                P.op("sync", lambda e: e.dma_start(out=dbg["WT24"], in_=B.WT[24][:]),
                     waits=allw, inc=s_d, inc_k=16)
                P.op("sync", lambda e: e.dma_start(out=dbg["ACC0"], in_=B.ACC[0][:]),
                     waits=allw, inc=s_d, inc_k=16)
            P.op("vector", lambda e, ct=ct, tc=tc, pmx=pmx: e.scalar_tensor_tensor(
                 B.RS[ct][:], pmx[:], 1.0, B.XB[ct][:, tc * 512:(tc + 1) * 512],
                 ALU.mult, ALU.add),
                 waits=[(s_t, s_t.n), (s_v, s_v.n), (s_s, s_s.n)], inc=s_v)
            P.op("vector", lambda e, ct=ct: e.tensor_scalar_add(
                 B.RS[ct][:], B.RS[ct][:], B.cols[:, ct:ct + 1]),
                 inc=s_v)
        rs_ready = [(s_v, s_v.n)]
        if dbg is not None and "CANARY" in dbg and tc == 0:
            P.op("vector", lambda e: e.memset(B.RS[0][:, 0:4], 7.0),
                 waits=rs_ready, inc=s_v)
            rs_ready = [(s_v, s_v.n)]
        if dbg is not None and "RS" in dbg and tc == 0:
            for ct in range(ND):
                P.op("sync", lambda e, ct=ct: e.dma_start(out=dbg["RS"][ct], in_=B.RS[ct][:]),
                     waits=rs_ready, inc=s_d, inc_k=16)
            rs_ready = rs_ready + [(s_d, s_d.n)]
        emit_ln(P, B, tc, rs_ready, B.XB)
    if dbg is not None and "XBend" in dbg:
        for ck in range(ND):
            P.op("sync", lambda e, ck=ck: e.dma_start(out=dbg["XBend"][ck], in_=B.XB[ck][:]),
                 waits=[(B.s_ln_ev, B.s_ln_ev.n)], inc=s_d, inc_k=16)
        fin = s_d.n
        P.raw("sync", lambda e, dedup: e.wait_ge(s_d.h, fin))
    return [(B.s_ln_ev, B.s_ln_ev.n)]



def emit_store_xb_as_f32(P, B, x_out, done):
    nc = P.nc
    stage = B.iostg
    orr = x_out.rearrange("(n p) d -> n p d", p=128)
    st_base = B.s_d.n
    for tt in range(T // 128):
        for ck in range(ND):
            pb = B.pt[ck % 2]
            P.op("tensor", lambda e, ck=ck, tt=tt, pb=pb: e.transpose(
                 pb[:, 0:128], B.XB[ck][:, tt * 128:(tt + 1) * 128], B.ident_bf[:]),
                 waits=list(done) + [(B.s_ev, B.s_ev.n - 1),
                                     (B.s_d, st_base + 16 * tt)],
                 inc=B.s_tp)
            P.op("vector", lambda e, ck=ck, pb=pb: e.tensor_copy(
                 stage[:, ck * 128:(ck + 1) * 128], pb[:, 0:128]),
                 waits=[(B.s_tp, B.s_tp.n)], inc=B.s_ev)
        P.op("sync", lambda e, tt=tt: e.dma_start(out=orr[tt], in_=stage[:]),
             waits=[(B.s_ev, B.s_ev.n)], inc=B.s_d, inc_k=16)
    fin = B.s_d.n
    P.raw("sync", lambda e, dedup: e.wait_ge(B.s_d.h, fin))


def build_phaseC_standalone(dbg_names=(), hp_list=None, do_out=True, tc_list=None):
    nc = bass.Bass("TRN2", target_bir_lowering=False, detect_race_conditions=RACE_DETECT)
    x_in = nc.dram_tensor("x_in", [T, D], F32, kind="ExternalInput").ap()
    x_out = nc.dram_tensor("x_out", [T, D], F32, kind="ExternalOutput").ap()
    W = {n: nc.dram_tensor(n, [D, D], F32, kind="ExternalInput").ap()
         for n in ("Wq", "Wk", "Wv", "Wout")}
    W["bout"] = nc.dram_tensor("bout", [D], F32, kind="ExternalInput").ap()
    cf = nc.dram_tensor("c_f32h", [128, CW], F32, kind="ExternalInput").ap()
    cb = nc.dram_tensor("c_bf16h", [128, CWB], BF16, kind="ExternalInput").ap()

    dbg = {}
    for nm in dbg_names:
        if nm in ("qT", "kT", "vT"):
            dbg[nm] = nc.dram_tensor("d_" + nm, [128, T], BF16, kind="ExternalOutput").ap()
        elif nm == "vtm":
            dbg[nm] = nc.dram_tensor("d_vtm", [T // 128, 128, 130], BF16, kind="ExternalOutput").ap()
        elif nm == "ACC0E":
            dbg[nm] = nc.dram_tensor("d_ACC0E", [128, T], BF16, kind="ExternalOutput").ap()
        elif nm == "ACC":
            dbg[nm] = [nc.dram_tensor(f"d_ACC{i}", [128, T], BF16, kind="ExternalOutput").ap()
                       for i in range(ND)]
        elif nm == "XB":
            dbg[nm] = [nc.dram_tensor(f"d_XB{i}", [128, T], BF16, kind="ExternalOutput").ap()
                       for i in range(ND)]
        elif nm == "XBend":
            dbg[nm] = [nc.dram_tensor(f"d_XE{i}", [128, T], BF16, kind="ExternalOutput").ap()
                       for i in range(ND)]
        elif nm == "RS":
            dbg[nm] = [nc.dram_tensor(f"d_RS{i}", [128, 512], F32, kind="ExternalOutput").ap()
                       for i in range(ND)]
        elif nm == "CANARY":
            dbg[nm] = True
        elif nm == "PM":
            dbg[nm] = nc.dram_tensor("d_PM", [128, 512], F32, kind="ExternalOutput").ap()
            dbg["WT24"] = nc.dram_tensor("d_WT24", [128, D], BF16, kind="ExternalOutput").ap()
            dbg["ACC0"] = nc.dram_tensor("d_ACC0x", [128, T], BF16, kind="ExternalOutput").ap()
    es = ExitStack()
    P = Prog(nc, es)
    with es:
        B = alloc_common(P, nc)
        load_consts(P, B, cf, cb)
        load_x_to_fm(P, x_in, B.XB, B.ident_bf, [p[:, 0:128] for p in B.pt],
                     B.s_g, B.s_tp, B.s_ev, extra=[B.consts_ready])
        x_ready = [(B.s_ev, B.s_ev.n), B.consts_ready]
        if "XB" in dbg:
            for ck in range(ND):
                P.op("sync", lambda e, ck=ck: e.dma_start(out=dbg["XB"][ck], in_=B.XB[ck][:]),
                     waits=x_ready, inc=B.s_d, inc_k=16)
            x_ready = x_ready + [(B.s_d, B.s_d.n)]
        done = emit_phaseC(P, B, W, x_ready, hp_list=hp_list, do_out=do_out,
                           dbg=(dbg if dbg else None), tc_list=tc_list)
        if do_out:
            emit_store_xb_as_f32(P, B, x_out, done)
        else:
            fin = B.s_d.n
            P.raw("sync", lambda e, dedup: e.wait_ge(B.s_d.h, fin))

        with nc.Block() as block:
            P.finalize(block)
    nc._P = P
    return nc



def emit_phaseD(P, B, W, prev_done, final_out=None, e_list=None):
    nc = P.nc
    s_t, s_v, s_s, s_g, s_d = B.s_t, B.s_v, B.s_s, B.s_g, B.s_d
    F = 4096
    NE = 8
    if e_list is None:
        e_list = range(NE)

    B.WT = [P.es.enter_context(nc.sbuf_tensor(f"WTD{i}", [128, D], BF16)) for i in range(32)]
    wg = P.es.enter_context(nc.sbuf_tensor("wgD", [128, NE * ND], BF16))
    bias_stage = P.es.enter_context(nc.sbuf_tensor("bstD", [NE, 128], F32))
    LG = P.es.enter_context(nc.sbuf_tensor("lgD", [NE, T], BF16))
    PRtm = P.es.enter_context(nc.sbuf_tensor("prD", [128, 16 * NE], F32))
    CW1 = P.es.enter_context(nc.sbuf_tensor("cw1D", [128, 16 * 5], F32))
    EQ = P.es.enter_context(nc.sbuf_tensor("eqD", [128, 16 * 2 * NE], F32))
    CF = P.es.enter_context(nc.sbuf_tensor("cfD", [NE, T], F32))
    HB = [P.es.enter_context(nc.sbuf_tensor(f"hbD{i}", [128, 512], BF16)) for i in range(ND)]
    b1c = P.es.enter_context(nc.sbuf_tensor("b1cD", [128, 4 * NE], F32))
    BT = P.es.enter_context(nc.sbuf_tensor("btD", [NE, D], F32))

    wgr = W["Wg"].rearrange("(k p) e -> k p e", p=128)
    for k in range(ND):
        P.op("gpsimd", lambda e, k=k: e.dma_start(out=wg[:, k * NE:(k + 1) * NE], in_=wgr[k]),
             waits=list(prev_done), inc=s_g, inc_k=16)
    P.op("sync", lambda e: e.dma_start(out=bias_stage[0:1, 0:NE], in_=W["bg"].rearrange("(a e) -> a e", a=1)),
         inc=s_d, inc_k=16)
    P.op("tensor", lambda e: e.transpose(B.pm[5][0:NE, 0:1], bias_stage[0:1, 0:NE],
                                         B.ident_f32[0:1, 0:1]),
         waits=[(s_d, s_d.n), (B.s_ev, B.s_ev.n), B.consts_ready], inc=B.s_tp)
    P.op("vector", lambda e: e.tensor_copy(B.cols[0:NE, 8:9], B.pm[5][0:NE, 0:1]),
         waits=[(B.s_tp, B.s_tp.n)], inc=B.s_ev)
    b1stg = P.es.enter_context(nc.sbuf_tensor("b1sD", [32, 128], F32))
    b1cols = P.es.enter_context(nc.sbuf_tensor("b1colD", [128, NE * 32], F32))
    for ei in range(NE):
        P.op("sync", lambda e, ei=ei: e.dma_start(
             out=b1stg[:], in_=W["b1"][ei].rearrange("(a p) -> a p", p=128)),
             waits=[(B.s_tp, B.s_tp.n)], inc=s_d, inc_k=16)
        P.op("tensor", lambda e, ei=ei: e.transpose(
             B.pm[5][:, 0:32], b1stg[:], B.ident_f32[0:32, 0:32]),
             waits=[(s_d, s_d.n), (B.s_ev, B.s_ev.n)], inc=B.s_tp)
        P.op("vector", lambda e, ei=ei: e.tensor_copy(
             b1cols[:, ei * 32:(ei + 1) * 32], B.pm[5][:, 0:32]),
             waits=[(B.s_tp, B.s_tp.n)], inc=B.s_ev)
    P.op("sync", lambda e: e.dma_start(out=BT[:], in_=W["b2"]), inc=s_d, inc_k=16)
    P.fence("sync", s_d)
    P.fence("gpsimd", s_g)
    smalls_done = (B.s_ev, B.s_ev.n)
    b2_done = (s_d, s_d.n)

    for tc in range(NT):
        pmx = B.pm[tc % 2]
        for k in range(ND):
            P.op("tensor", lambda e, k=k, tc=tc, pmx=pmx: nc.tensor.matmul(
                 pmx[0:NE, :], wg[:, k * NE:(k + 1) * NE],
                 B.XB[k][:, tc * 512:(tc + 1) * 512],
                 start=(k == 0), stop=(k == ND - 1)),
                 waits=list(prev_done) + [(s_g, s_g.n), smalls_done, (s_s, s_s.n - 1)],
                 inc=s_t)
        P.op("scalar", lambda e, tc=tc, pmx=pmx: e.add(
             LG[:, tc * 512:(tc + 1) * 512], pmx[0:NE, :], B.cols[0:NE, 8:9]),
             waits=[(s_t, s_t.n)], inc=s_s)
    lg_done = (s_s, s_s.n)
    for tt in range(T // 128):
        P.op("tensor", lambda e, tt=tt: e.transpose(
             B.pt[0][0:128, 0:NE], LG[0:NE, tt * 128:(tt + 1) * 128],
             B.ident_bf[0:NE, 0:NE]),
             waits=[lg_done, (s_v, s_v.n)], inc=B.s_tp)
        P.op("vector", lambda e, tt=tt: e.tensor_copy(
             PRtm[:, tt * NE:(tt + 1) * NE], B.pt[0][0:128, 0:NE]),
             waits=[(B.s_tp, B.s_tp.n)], inc=s_v)
    NTT = T // 128
    prs = [PRtm[:, tt * NE:(tt + 1) * NE] for tt in range(NTT)]
    cwc = lambda tt, j: CW1[:, tt * 5 + j:tt * 5 + j + 1]
    eq1s = [EQ[:, tt * 16:tt * 16 + NE] for tt in range(NTT)]
    eq2s = [EQ[:, tt * 16 + NE:tt * 16 + 2 * NE] for tt in range(NTT)]
    def vstepD(fn):
        for tt in range(NTT):
            P.op("vector", lambda e, tt=tt: fn(e, tt), inc=s_v)
    vstepD(lambda e, tt: e.reduce_max(cwc(tt, 4), prs[tt], AX.X, negate=True))
    ex_w = (s_v, s_v.n)
    for tt in range(NTT):
        P.op("scalar", lambda e, tt=tt: e.activation(prs[tt], prs[tt], AF.Exp,
                                                     bias=cwc(tt, 4)),
             waits=[ex_w], inc=s_s)
    sm_w = (s_s, s_s.n)
    for tt in range(NTT):
        P.op("vector", lambda e, tt=tt: e.reduce_sum(cwc(tt, 1), prs[tt], AX.X),
             waits=[sm_w], inc=s_v)
    vstepD(lambda e, tt: e.reciprocal(cwc(tt, 2), cwc(tt, 1)))
    vstepD(lambda e, tt: e.tensor_scalar_mul(prs[tt], prs[tt], cwc(tt, 2)))
    vstepD(lambda e, tt: e.reduce_max(cwc(tt, 0), prs[tt], AX.X))
    vstepD(lambda e, tt: e.tensor_scalar(eq1s[tt], prs[tt], cwc(tt, 0), None, ALU.is_equal))
    vstepD(lambda e, tt: e.scalar_tensor_tensor(eq2s[tt], eq1s[tt], -1e30, prs[tt],
                                                ALU.mult, ALU.add))
    vstepD(lambda e, tt: e.reduce_max(cwc(tt, 1), eq2s[tt], AX.X))
    vstepD(lambda e, tt: e.tensor_scalar(eq2s[tt], prs[tt], cwc(tt, 1), None, ALU.is_equal))
    vstepD(lambda e, tt: e.tensor_add(cwc(tt, 2), cwc(tt, 0), cwc(tt, 1)))
    vstepD(lambda e, tt: e.tensor_scalar_add(cwc(tt, 2), cwc(tt, 2), 1e-8))
    vstepD(lambda e, tt: e.reciprocal(cwc(tt, 3), cwc(tt, 2)))
    vstepD(lambda e, tt: e.tensor_scalar_mul(eq1s[tt], eq1s[tt], cwc(tt, 0)))
    vstepD(lambda e, tt: e.tensor_scalar_mul(eq2s[tt], eq2s[tt], cwc(tt, 1)))
    vstepD(lambda e, tt: e.tensor_add(prs[tt], eq1s[tt], eq2s[tt]))
    vstepD(lambda e, tt: e.tensor_scalar_mul(prs[tt], prs[tt], cwc(tt, 3)))
    cdone = (s_v, s_v.n)
    for tt in range(NTT):
        P.op("tensor", lambda e, tt=tt: e.transpose(
             B.pm[2][0:NE, 0:128], prs[tt], B.ident_f32[:]),
             waits=[cdone, (B.s_ev, B.s_ev.n)], inc=B.s_tp)
        P.op("vector", lambda e, tt=tt: e.tensor_copy(
             CF[:, tt * 128:(tt + 1) * 128], B.pm[2][0:NE, 0:128]),
             waits=[(B.s_tp, B.s_tp.n)], inc=B.s_ev)
    cf_done = (B.s_ev, B.s_ev.n)
    if W.get("d_CF") is not None:
        P.op("sync", lambda e: e.dma_start(out=W["d_CF"], in_=CF[:]),
             waits=[cf_done], inc=s_d, inc_k=16)

    yv = [B.ACC[i][:].bitcast(F32) for i in range(ND)]
    w1r = [W["W1"][ei].rearrange("(k p) f -> k p f", p=128) for ei in range(NE)]
    w2r = [W["W2"][ei].rearrange("(k p) c -> k p c", p=128) for ei in range(NE)]
    last_use = {0: (s_t, 0), 1: (s_t, 0)}
    piece = 0
    dma_done = {}
    for th in range(2):
        ymemset = []
        for ct in range(ND):
            P.op("vector", lambda e, ct=ct: e.memset(yv[ct], 0.0),
                 waits=[cf_done, (B.s_ln_ev, B.s_ln_ev.n)], inc=s_v)
        yzero = (s_v, s_v.n)
        for ei in e_list:
            for tc2 in range(2):
                tcg = th * 2 + tc2
                P.op("tensor", lambda e, ei=ei, tcg=tcg, tc2=tc2: nc.tensor.matmul(
                     B.pm[4 + tc2][:], B.sel8[ei], CF[0:8, tcg * 512:(tcg + 1) * 512],
                     start=True, stop=True),
                     waits=[cf_done, (s_v, s_v.n), (s_s, s_s.n),
                            (B.s_ln_ev, B.s_ln_ev.n)], inc=s_t)
            cb_done = (s_t, s_t.n)
            for fp in range(4):
                pb = piece % 2
                for k in range(ND):
                    P.op("gpsimd", lambda e, ei=ei, fp=fp, k=k, pb=pb: e.dma_start(
                         out=B.WT[pb * 16 + k][:],
                         in_=w1r[ei][k][:, fp * 1024:(fp + 1) * 1024]),
                         waits=[last_use[pb]], inc=s_g, inc_k=16)
                for j in range(ND):
                    P.op("gpsimd", lambda e, ei=ei, fp=fp, j=j, pb=pb: e.dma_start(
                         out=B.WT[pb * 16 + 8 + j][:],
                         in_=w2r[ei][fp * 8 + j]),
                         waits=[last_use[pb]], inc=s_g, inc_k=16)
                P.fence("gpsimd", s_g)
                wrdy = (s_g, s_g.n)
                for tc2 in range(2):
                    tcg = th * 2 + tc2
                    for ft in range(ND):
                        pmx = B.pm[ft % 2]
                        for k in range(ND):
                            P.op("tensor", lambda e, k=k, ft=ft, tcg=tcg, pb=pb, pmx=pmx:
                                 nc.tensor.matmul(pmx[:],
                                                  B.WT[pb * 16 + k][:, ft * 128:(ft + 1) * 128],
                                                  B.XB[k][:, tcg * 512:(tcg + 1) * 512],
                                                  start=(k == 0), stop=(k == ND - 1)),
                                 waits=[wrdy, cb_done, (s_v, s_v.n - 1), (s_s, s_s.n - 1),
                                        (B.s_ev, B.s_ev.n)],
                                 inc=s_t)
                        P.op("scalar", lambda e, ft=ft, ei=ei, fp=fp, pmx=pmx:
                             e.activation(HB[ft][:], pmx[:], AF.Gelu_apprx_tanh,
                                          bias=b1cols[:, ei * 32 + fp * 8 + ft:ei * 32 + fp * 8 + ft + 1]),
                             waits=[(s_t, s_t.n)], inc=s_s)
                        P.op("vector", lambda e, ft=ft, tc2=tc2: e.tensor_mul(
                             HB[ft][:], HB[ft][:], B.pm[4 + tc2][:]),
                             waits=[(s_s, s_s.n), yzero], inc=s_v)
                    h_done = (s_v, s_v.n)
                    for ct in range(ND):
                        pmx = B.pm[2 + ct % 2]
                        for j in range(ND):
                            P.op("tensor", lambda e, j=j, ct=ct, pb=pb, pmx=pmx:
                                 nc.tensor.matmul(pmx[:],
                                                  B.WT[pb * 16 + 8 + j][:, ct * 128:(ct + 1) * 128],
                                                  HB[j][:],
                                                  start=(j == 0), stop=(j == ND - 1)),
                                 waits=[h_done, (s_v, s_v.n - 1)],
                                 inc=s_t)
                        P.op("vector", lambda e, ct=ct, tc2=tc2, pmx=pmx: e.tensor_add(
                             yv[ct][:, tc2 * 512:(tc2 + 1) * 512],
                             yv[ct][:, tc2 * 512:(tc2 + 1) * 512], pmx[:]),
                             waits=[(s_t, s_t.n)], inc=s_v)
                last_use[pb] = (s_t, s_t.n)
                piece += 1
        for tc2 in range(2):
            tcg = th * 2 + tc2
            for ct in range(ND):
                pmx = B.pm[2 + ct % 2]
                P.op("tensor", lambda e, ct=ct, tcg=tcg, pmx=pmx: nc.tensor.matmul(
                     pmx[:], BT[:, ct * 128:(ct + 1) * 128],
                     CF[:, tcg * 512:(tcg + 1) * 512],
                     start=True, stop=True),
                     waits=[b2_done, (s_v, s_v.n - 1)], inc=s_t)
                P.op("vector", lambda e, ct=ct, tc2=tc2, pmx=pmx: e.tensor_add(
                     yv[ct][:, tc2 * 512:(tc2 + 1) * 512],
                     yv[ct][:, tc2 * 512:(tc2 + 1) * 512], pmx[:]),
                     waits=[(s_t, s_t.n)], inc=s_v)
        for tc2 in range(2):
            tcg = th * 2 + tc2
            for ct in range(ND):
                P.op("vector", lambda e, ct=ct, tc2=tc2, tcg=tcg: e.tensor_add(
                     B.RS[ct][:], yv[ct][:, tc2 * 512:(tc2 + 1) * 512],
                     B.XB[ct][:, tcg * 512:(tcg + 1) * 512]),
                     waits=[(s_v, s_v.n)], inc=s_v)
            rs_ready = [(s_v, s_v.n)]
            if final_out is not None:
                emit_ln(P, B, tcg, rs_ready, None, out_f32_tiles="rs")
                emit_store_rs_f32(P, B, final_out, tcg)
            else:
                emit_ln(P, B, tcg, rs_ready, B.XB)
    if final_out is not None:
        fin = B.s_d.n
        P.raw("sync", lambda e, dedup: e.wait_ge(B.s_d.h, fin))
        return [(B.s_d, fin)]
    return [(B.s_ln_ev, B.s_ln_ev.n)]


def emit_store_rs_f32(P, B, out_hbm, tcg):
    nc = P.nc
    if not hasattr(B, "st_cnt"):
        B.st_cnt = [0]
    stage = B.iostg
    orr = out_hbm.rearrange("(n p) d -> n p d", p=128)
    rs_done = (B.s_ln_ev, B.s_ln_ev.n)
    for q in range(4):
        tt = tcg * 4 + q
        for ck in range(ND):
            pmx = B.pm[ck % 2]
            P.op("tensor", lambda e, ck=ck, q=q, pmx=pmx: e.transpose(
                 pmx[:, 0:128], B.RS[ck][:, q * 128:(q + 1) * 128], B.ident_f32[:]),
                 waits=[rs_done, (B.s_ev, B.s_ev.n - 1), (B.s_d, B.st_cnt[0])],
                 inc=B.s_tp)
            P.op("vector", lambda e, ck=ck, pmx=pmx: e.tensor_copy(
                 stage[:, ck * 128:(ck + 1) * 128], pmx[:, 0:128]),
                 waits=[(B.s_tp, B.s_tp.n)], inc=B.s_ev)
        P.op("sync", lambda e, tt=tt: e.dma_start(out=orr[tt], in_=stage[:]),
             waits=[(B.s_ev, B.s_ev.n)], inc=B.s_d, inc_k=16)
        B.st_cnt[0] = B.s_d.n



def build_phaseD_standalone(final=True, dump_cf=False):
    nc = bass.Bass("TRN2", target_bir_lowering=False, detect_race_conditions=RACE_DETECT)
    x_in = nc.dram_tensor("x_in", [T, D], F32, kind="ExternalInput").ap()
    x_out = nc.dram_tensor("x_out", [T, D], F32, kind="ExternalOutput").ap()
    W = {
        "Wg": nc.dram_tensor("Wg", [D, 8], F32, kind="ExternalInput").ap(),
        "bg": nc.dram_tensor("bg", [8], F32, kind="ExternalInput").ap(),
        "W1": nc.dram_tensor("W1", [8, D, 4096], F32, kind="ExternalInput").ap(),
        "b1": nc.dram_tensor("b1", [8, 4096], F32, kind="ExternalInput").ap(),
        "W2": nc.dram_tensor("W2", [8, 4096, D], F32, kind="ExternalInput").ap(),
        "b2": nc.dram_tensor("b2", [8, D], F32, kind="ExternalInput").ap(),
    }
    cf = nc.dram_tensor("c_f32h", [128, CW], F32, kind="ExternalInput").ap()
    cb = nc.dram_tensor("c_bf16h", [128, CWB], BF16, kind="ExternalInput").ap()
    if dump_cf:
        W["d_CF"] = nc.dram_tensor("d_CF", [8, T], F32, kind="ExternalOutput").ap()
    es = ExitStack()
    P = Prog(nc, es)
    with es:
        B = alloc_common(P, nc)
        load_consts(P, B, cf, cb)
        load_x_to_fm(P, x_in, B.XB, B.ident_bf, [p[:, 0:128] for p in B.pt],
                     B.s_g, B.s_tp, B.s_ev, extra=[B.consts_ready])
        x_ready = [(B.s_ev, B.s_ev.n), B.consts_ready]
        emit_phaseD(P, B, W, x_ready, final_out=x_out)
        with nc.Block() as block:
            P.finalize(block)
    nc._P = P
    return nc



def emit_phaseB(P, B, W, prev_done, stage=99):
    nc = P.nc
    s_t, s_v, s_s, s_g, s_d = B.s_t, B.s_v, B.s_s, B.s_g, B.s_d
    NE = 8

    for i, nm in enumerate(("ln_g", "ln_b", "bn_mean", "bn_var", "bn_g", "bn_b")):
        load_vec_cols(P, W[nm], D, B.cols[:, i * 8:(i + 1) * 8], B.ident_f32, B.pm[5],
                      s_d, B.s_tp, B.s_ev)
    B.WT = [P.es.enter_context(nc.sbuf_tensor(f"WTB{i}", [128, D], BF16)) for i in range(16)]
    wr = P.es.enter_context(nc.sbuf_tensor("wrB", [128, NE * ND], F32))
    wrr = W["Wr"].rearrange("(k p) e -> k p e", p=128)
    for k in range(ND):
        P.op("gpsimd", lambda e, k=k: e.dma_start(out=wr[:, k * NE:(k + 1) * NE], in_=wrr[k]),
             waits=list(prev_done), inc=s_g, inc_k=16)
    brs = P.es.enter_context(nc.sbuf_tensor("brB", [NE, 130], F32))
    P.op("sync", lambda e: e.dma_start(out=brs[0:1, 0:NE],
                                       in_=W["br"].rearrange("(a e) -> a e", a=1)),
         inc=s_d, inc_k=16)
    P.op("sync", lambda e: e.dma_start(out=brs[0:NE, 64:128], in_=W["b_down"]),
         inc=s_d, inc_k=16)
    bdc = P.es.enter_context(nc.sbuf_tensor("bdcB", [128, 10], F32))
    P.op("tensor", lambda e: e.transpose(B.pm[5][0:64, 0:NE], brs[0:NE, 64:128],
                                         B.ident_f32[0:NE, 0:NE]),
         waits=[(s_d, s_d.n), (B.s_ev, B.s_ev.n), B.consts_ready], inc=B.s_tp)
    P.op("vector", lambda e: e.tensor_copy(bdc[0:64, 0:NE], B.pm[5][0:64, 0:NE]),
         waits=[(B.s_tp, B.s_tp.n)], inc=B.s_ev)
    BU = P.es.enter_context(nc.sbuf_tensor("buB", [NE, D], F32))
    P.op("sync", lambda e: e.dma_start(out=BU[:], in_=W["b_up"]), inc=s_d, inc_k=16)
    wdr = [W["W_down"][ei].rearrange("(k p) h -> k p h", p=128) for ei in range(NE)]
    for ei in range(NE):
        for k in range(ND):
            P.op("gpsimd", lambda e, ei=ei, k=k: e.dma_start(
                 out=B.WT[k][:, ei * 64:(ei + 1) * 64], in_=wdr[ei][k]),
                 waits=list(prev_done), inc=s_g, inc_k=16)
        P.op("gpsimd", lambda e, ei=ei: e.dma_start(
             out=B.WT[8 + ei][0:64, :], in_=W["W_up"][ei]),
             waits=list(prev_done), inc=s_g, inc_k=16)
    P.fence("gpsimd", s_g)
    P.fence("sync", s_d)
    wts_done = (s_g, s_g.n)
    smalls_done = (B.s_ev, B.s_ev.n)

    for ct in range(ND):
        P.op("vector", lambda e, ct=ct: e.memset(B.cols[:, 48 + ct:49 + ct], 0.0),
             waits=[smalls_done], inc=s_v)
    g_cols = [B.cols[:, ct:ct + 1] for ct in range(ND)]
    b_cols = [B.cols[:, 8 + ct:9 + ct] for ct in range(ND)]
    for tc in range(NT):
        for ct in range(ND):
            P.op("vector", lambda e, ct=ct, tc=tc: e.tensor_copy(
                 B.RS[ct][:], B.XB[ct][:, tc * 512:(tc + 1) * 512]),
                 waits=list(prev_done) + [(B.s_ln_ev, B.s_ln_ev.n), (s_v, s_v.n)],
                 inc=s_v)
        rs_ready = [(s_v, s_v.n)]
        emit_ln(P, B, tc, rs_ready, B.ACC, out_f32_tiles="rs",
                gain_col=g_cols, bias_col=b_cols)
        lnr = (B.s_ln_ev, B.s_ln_ev.n)
        for ct in range(ND):
            P.op("vector", lambda e, ct=ct: e.reduce_sum(
                 B.cols[:, 56:57], B.RS[ct][:], AX.X),
                 waits=[lnr, (s_v, s_v.n)], inc=s_v)
            P.op("vector", lambda e, ct=ct: e.tensor_add(
                 B.cols[:, 48 + ct:49 + ct], B.cols[:, 48 + ct:49 + ct],
                 B.cols[:, 56:57]), waits=[(s_v, s_v.n)], inc=s_v)
    z_done = (s_v, s_v.n)
    acc_done = (B.s_ln_ev, B.s_ln_ev.n)
    if stage <= 1:
        return [z_done, acc_done, (s_g, s_g.n), (s_d, s_d.n), (B.s_ev, B.s_ev.n)]

    def cstep(fn, waits=()):
        P.op("vector", fn, waits=list(waits) + [(s_v, s_v.n)], inc=s_v)
    for ct in range(ND):
        c = B.cols[:, 48 + ct:49 + ct]
        cstep(lambda e, c=c: e.tensor_scalar_mul(c, c, 1.0 / T), [z_done])
        cstep(lambda e, c=c, ct=ct: e.tensor_mul(c, c, B.cols[:, ct:ct + 1]))
        cstep(lambda e, c=c, ct=ct: e.tensor_add(c, c, B.cols[:, 8 + ct:9 + ct]))
        cstep(lambda e, c=c, ct=ct: e.tensor_sub(c, c, B.cols[:, 16 + ct:17 + ct]))
    for ct in range(ND):
        P.op("scalar", lambda e, ct=ct: e.activation(
             B.cols[:, 56:57], B.cols[:, 24 + ct:25 + ct], AF.Sqrt,
             bias=B.eps5_col), waits=[(s_v, s_v.n)], inc=s_s)
        P.op("vector", lambda e, ct=ct: e.reciprocal(B.cols[:, 57:58], B.cols[:, 56:57]),
             waits=[(s_s, s_s.n)], inc=s_v)
        c = B.cols[:, 48 + ct:49 + ct]
        cstep(lambda e, c=c: e.tensor_mul(c, c, B.cols[:, 57:58]))
        cstep(lambda e, c=c, ct=ct: e.tensor_mul(c, c, B.cols[:, 32 + ct:33 + ct]))
        cstep(lambda e, c=c, ct=ct: e.tensor_add(c, c, B.cols[:, 40 + ct:41 + ct]))
    zbn_done = (s_v, s_v.n)
    if stage <= 2:
        return [zbn_done, acc_done, (s_g, s_g.n), (s_d, s_d.n), (B.s_ev, B.s_ev.n)]
    for k in range(ND):
        P.op("tensor", lambda e, k=k: nc.tensor.matmul(
             B.pm[4][0:NE, 0:1], wr[:, k * NE:(k + 1) * NE],
             B.cols[:, 48 + k:49 + k], start=(k == 0), stop=(k == ND - 1)),
             waits=[zbn_done, (s_g, s_g.n), (B.s_ln_ev, B.s_ln_ev.n)], inc=s_t)
    P.op("vector", lambda e: e.tensor_copy(brs[0:NE, 128:129], B.pm[4][0:NE, 0:1]),
         waits=[(s_t, s_t.n)], inc=s_v)
    P.op("tensor", lambda e: e.transpose(B.pm[4][0:1, 0:NE], brs[0:NE, 128:129],
                                         B.ident_f32[0:NE, 0:NE]),
         waits=[(s_v, s_v.n)], inc=s_t)
    pro = brs[0:1, 8:16]
    def rstep(fn, eng="vector", waits=()):
        if eng == "vector":
            P.op("vector", fn, waits=list(waits) + [(s_v, s_v.n), (s_s, s_s.n)], inc=s_v)
        else:
            P.op("scalar", fn, waits=list(waits) + [(s_v, s_v.n), (s_s, s_s.n)], inc=s_s)
    rstep(lambda e: e.tensor_add(pro, B.pm[4][0:1, 0:NE], brs[0:1, 0:NE]),
          waits=[(s_t, s_t.n), (s_d, s_d.n)])
    rstep(lambda e: e.reduce_max(brs[0:1, 16:17], pro, AX.X, negate=True))
    rstep(lambda e: e.activation(pro, pro, AF.Exp, bias=brs[0:1, 16:17]), eng="scalar")
    rstep(lambda e: e.reduce_sum(brs[0:1, 17:18], pro, AX.X))
    rstep(lambda e: e.reciprocal(brs[0:1, 18:19], brs[0:1, 17:18]))
    rstep(lambda e: e.tensor_scalar_mul(pro, pro, brs[0:1, 18:19]))
    rstep(lambda e: e.reduce_max(brs[0:1, 16:17], pro, AX.X))
    rstep(lambda e: e.tensor_scalar(brs[0:1, 24:32], pro, brs[0:1, 16:17], None,
                                    ALU.is_equal))
    rstep(lambda e: e.tensor_scalar_mul(brs[0:1, 24:32], brs[0:1, 24:32],
                                        brs[0:1, 16:17]))
    comb_done = (s_v, s_v.n)
    P.op("tensor", lambda e: e.transpose(B.pm[4][0:NE, 0:1], brs[0:1, 24:32],
                                         B.ident_f32[0:1, 0:1]),
         waits=[comb_done], inc=s_t)
    P.op("vector", lambda e: e.tensor_copy(brs[0:NE, 129:130], B.pm[4][0:NE, 0:1]),
         waits=[(s_t, s_t.n)], inc=s_v)
    cc_done = (s_v, s_v.n)
    ecols = P.es.enter_context(nc.sbuf_tensor("ecolB", [128, NE], F32))
    bucols = P.es.enter_context(nc.sbuf_tensor("bucB", [128, ND], F32))
    for ei in range(NE):
        P.op("tensor", lambda e, ei=ei: nc.tensor.matmul(
             B.pm[4][:, 0:1], B.sel8[ei], brs[0:NE, 129:130],
             start=True, stop=True),
             waits=[cc_done, (s_v, s_v.n)], inc=s_t)
        P.op("vector", lambda e, ei=ei: e.tensor_copy(ecols[:, ei:ei + 1], B.pm[4][:, 0:1]),
             waits=[(s_t, s_t.n)], inc=s_v)
    for ct in range(ND):
        P.op("tensor", lambda e, ct=ct: nc.tensor.matmul(
             B.pm[4][:, 0:1], BU[:, ct * 128:(ct + 1) * 128], brs[0:NE, 129:130],
             start=True, stop=True),
             waits=[cc_done, (s_v, s_v.n), (s_d, s_d.n)], inc=s_t)
        P.op("vector", lambda e, ct=ct: e.tensor_copy(bucols[:, ct:ct + 1], B.pm[4][:, 0:1]),
             waits=[(s_t, s_t.n)], inc=s_v)
    ecols_done = (s_v, s_v.n)
    if stage <= 3:
        return [ecols_done, acc_done, (s_g, s_g.n), (s_d, s_d.n), (B.s_ev, B.s_ev.n)]

    HB2 = [P.es.enter_context(nc.sbuf_tensor(f"hb2B{i}", [64, 512], BF16)) for i in range(NE)]
    for tc in range(NT):
        for ei in range(NE):
            pmx = B.pm[ei % 2]
            for k in range(ND):
                P.op("tensor", lambda e, ei=ei, k=k, tc=tc, pmx=pmx: nc.tensor.matmul(
                     pmx[0:64, :], B.WT[k][:, ei * 64:(ei + 1) * 64],
                     B.ACC[k][:, tc * 512:(tc + 1) * 512],
                     start=(k == 0), stop=(k == ND - 1)),
                     waits=[acc_done, wts_done, (s_s, s_s.n - 1), (s_v, s_v.n - 1)],
                     inc=s_t)
            P.op("scalar", lambda e, ei=ei, pmx=pmx: e.activation(
                 B.SQ[7][0:64, :], pmx[0:64, :], AF.Relu, bias=bdc[0:64, ei:ei + 1]),
                 waits=[(s_t, s_t.n), (s_v, s_v.n)], inc=s_s)
            P.op("vector", lambda e, ei=ei: e.tensor_scalar_mul(
                 HB2[ei][0:64, :], B.SQ[7][0:64, :],
                 ecols[0:64, ei:ei + 1]),
                 waits=[(s_s, s_s.n), ecols_done], inc=s_v)
        h_done = (s_v, s_v.n)
        if stage <= 4:
            continue
        for ct in range(ND):
            pmx = B.pm[2 + ct % 2]
            for ei in range(NE):
                P.op("tensor", lambda e, ei=ei, ct=ct, pmx=pmx: nc.tensor.matmul(
                     pmx[:], B.WT[8 + ei][0:64, ct * 128:(ct + 1) * 128],
                     HB2[ei][0:64, :],
                     start=(ei == 0), stop=(ei == NE - 1)),
                     waits=[h_done, (s_v, s_v.n - 1)], inc=s_t)
            P.op("vector", lambda e, ct=ct, tc=tc, pmx=pmx: e.scalar_tensor_tensor(
                 B.RS[ct][:], pmx[:], 1.0, B.XB[ct][:, tc * 512:(tc + 1) * 512],
                 ALU.mult, ALU.add),
                 waits=[(s_t, s_t.n), (s_v, s_v.n)], inc=s_v)
            P.op("vector", lambda e, ct=ct: e.tensor_scalar_add(
                 B.RS[ct][:], B.RS[ct][:], bucols[:, ct:ct + 1]), inc=s_v)
        rs_ready = [(s_v, s_v.n)]
        emit_ln(P, B, tc, rs_ready, B.XB)
    return [(B.s_ln_ev, B.s_ln_ev.n)]


def build_phaseB_standalone(stage=99):
    nc = bass.Bass("TRN2", target_bir_lowering=False, detect_race_conditions=RACE_DETECT)
    x_in = nc.dram_tensor("x_in", [T, D], F32, kind="ExternalInput").ap()
    x_out = nc.dram_tensor("x_out", [T, D], F32, kind="ExternalOutput").ap()
    W = {
        "ln_g": nc.dram_tensor("ln_g", [D], F32, kind="ExternalInput").ap(),
        "ln_b": nc.dram_tensor("ln_b", [D], F32, kind="ExternalInput").ap(),
        "bn_g": nc.dram_tensor("bn_g", [D], F32, kind="ExternalInput").ap(),
        "bn_b": nc.dram_tensor("bn_b", [D], F32, kind="ExternalInput").ap(),
        "bn_mean": nc.dram_tensor("bn_mean", [D], F32, kind="ExternalInput").ap(),
        "bn_var": nc.dram_tensor("bn_var", [D], F32, kind="ExternalInput").ap(),
        "Wr": nc.dram_tensor("Wr", [D, 8], F32, kind="ExternalInput").ap(),
        "br": nc.dram_tensor("br", [8], F32, kind="ExternalInput").ap(),
        "W_down": nc.dram_tensor("W_down", [8, D, 64], F32, kind="ExternalInput").ap(),
        "b_down": nc.dram_tensor("b_down", [8, 64], F32, kind="ExternalInput").ap(),
        "W_up": nc.dram_tensor("W_up", [8, 64, D], F32, kind="ExternalInput").ap(),
        "b_up": nc.dram_tensor("b_up", [8, D], F32, kind="ExternalInput").ap(),
    }
    cf = nc.dram_tensor("c_f32h", [128, CW], F32, kind="ExternalInput").ap()
    cb = nc.dram_tensor("c_bf16h", [128, CWB], BF16, kind="ExternalInput").ap()
    es = ExitStack()
    P = Prog(nc, es)
    with es:
        B = alloc_common(P, nc)
        load_consts(P, B, cf, cb)
        load_x_to_fm(P, x_in, B.XB, B.ident_bf, [p[:, 0:128] for p in B.pt],
                     B.s_g, B.s_tp, B.s_ev, extra=[B.consts_ready])
        x_ready = [(B.s_ev, B.s_ev.n), B.consts_ready]
        done = emit_phaseB(P, B, W, x_ready, stage=stage)
        emit_store_xb_as_f32(P, B, x_out, done)
        with nc.Block() as block:
            P.finalize(block)
    nc._P = P
    return nc



def emit_phaseA(P, B, W, prev_done, head_list=None, dbg=None, stageA=99, scanp=15, gp=15, acc_out=None, acc_in=None, stage_late=99):
    nc = P.nc
    s_t, s_v, s_s, s_g, s_d = B.s_t, B.s_v, B.s_s, B.s_g, B.s_d
    H = 4
    DK = 256
    NCH = 16
    PAD = 63
    if head_list is None:
        head_list = range(H)

    es = P.es
    W1S = [es.enter_context(nc.sbuf_tensor(f"w1sA{i}", [128, D], BF16)) for i in range(2)]
    W1B = es.enter_context(nc.sbuf_tensor("w1bA", [16, D], BF16))
    W2T = es.enter_context(nc.sbuf_tensor("w2tA", [128, 4 * ND], BF16))
    WBT = es.enter_context(nc.sbuf_tensor("wbtA", [128, 4 * ND], BF16))
    CV = es.enter_context(nc.sbuf_tensor("cvA", [128, 96], F32))
    FS = es.enter_context(nc.sbuf_tensor("fsA", [128, 40], F32))
    FL = es.enter_context(nc.sbuf_tensor("flA", [128, 512], F32))
    B1C = es.enter_context(nc.sbuf_tensor("b1cA", [128, ND], F32))
    SM = es.enter_context(nc.sbuf_tensor("smA", [128, 16], F32))
    RW = es.enter_context(nc.sbuf_tensor("rwA", [128, 256], F32))
    BETA = es.enter_context(nc.sbuf_tensor("betaA", [4, T], BF16))
    BTM = es.enter_context(nc.sbuf_tensor("btmA", [128, 64], F32))
    STF = es.enter_context(nc.sbuf_tensor("stfA", [16, T], BF16))
    LGA = es.enter_context(nc.sbuf_tensor("lgaA", [4, T], F32))
    LGF = LGA
    PR4 = es.enter_context(nc.sbuf_tensor("pr4A", [128, 64], F32))
    QN = es.enter_context(nc.sbuf_tensor("qnA", [128, 2 * T], BF16))
    KN = es.enter_context(nc.sbuf_tensor("knA", [128, 2 * T], BF16))
    VS = es.enter_context(nc.sbuf_tensor("vsA", [128, 2 * 2112], BF16))
    KTM = es.enter_context(nc.sbuf_tensor("ktmA", [128, 2 * T], BF16))
    UTM = es.enter_context(nc.sbuf_tensor("utmA", [128, 2 * T], BF16))
    RSB = [B.RS[j][:].bitcast(BF16) for j in range(8)]
    def vtm_t(i):
        off = i * 256
        return RSB[off // 1024][:, off % 1024: off % 1024 + 256]
    def wtf_s(s, i):
        off = s * T + i * 128
        return RSB[4 + off // 1024][:, off % 1024: off % 1024 + 128]
    PRE = es.enter_context(nc.sbuf_tensor("preA", [128, 2112], BF16))
    CACC = es.enter_context(nc.sbuf_tensor("caccA", [128, T], F32))
    WQS = [es.enter_context(nc.sbuf_tensor(f"wqsA{i}", [128, 256], BF16)) for i in range(ND)]
    INVT = es.enter_context(nc.sbuf_tensor("invtA", [128, 128], BF16))
    UIB = es.enter_context(nc.sbuf_tensor("uibA", [128, 256], BF16))
    SSB = es.enter_context(nc.sbuf_tensor("ssbA", [128, 512], BF16))
    SSB32 = es.enter_context(nc.sbuf_tensor("ssb32A", [128, 512], F32))
    ATT = es.enter_context(nc.sbuf_tensor("attA", [128, 128], BF16))
    KBG = es.enter_context(nc.sbuf_tensor("kbgA", [128, 256], BF16))
    KBT = es.enter_context(nc.sbuf_tensor("kbtA", [128, 256], BF16))
    SCS = es.enter_context(nc.sbuf_tensor("scsA", [128, 256], F32))
    STT = es.enter_context(nc.sbuf_tensor("sttA", [128, 16], F32))
    DO = [B.SQ[j][:].bitcast(BF16) for j in range(4)]
    VT2 = [B.RS[j][:].bitcast(BF16) for j in range(4)]
    LS = QN
    LL = KN
    OT = UTM

    def do_tile(i):
        return DO[i // 4][:, (i % 4) * 256:(i % 4) * 256 + 256]

    Mv = CACC[:, 0:128]
    Mtv = CACC[:, 128:256]
    Rv = CACC[:, 256:384]
    R2v = CACC[:, 384:512]

    P.op("gpsimd", lambda e: e.dma_start(out=W1B[:], in_=W["mlp_w1"][D:D + 16, :]),
         waits=list(prev_done), inc=s_g, inc_k=16)
    P.op("gpsimd", lambda e: e.dma_start(
         out=W2T[:].rearrange("p (k j) -> p k j", k=ND), in_=W["mlp_w2"].rearrange("(k p) j -> p k j", p=128)),
         waits=list(prev_done), inc=s_g, inc_k=16)
    P.op("gpsimd", lambda e: e.dma_start(
         out=WBT[:].rearrange("p (k j) -> p k j", k=ND), in_=W["Wb"].rearrange("(k p) j -> p k j", p=128)),
         waits=list(prev_done), inc=s_g, inc_k=16)
    P.op("sync", lambda e: e.dma_start(
         out=CV[:, 0:32].rearrange("p (k j) -> p k j", k=ND), in_=W["conv_w_q"].rearrange("(k p) j -> p k j", p=128)),
         waits=list(prev_done), inc=s_d, inc_k=16)
    P.op("sync", lambda e: e.dma_start(
         out=CV[:, 32:64].rearrange("p (k j) -> p k j", k=ND), in_=W["conv_w_k"].rearrange("(k p) j -> p k j", p=128)),
         inc=s_d, inc_k=16)
    P.op("sync", lambda e: e.dma_start(
         out=CV[:, 64:96].rearrange("p (k j) -> p k j", k=ND), in_=W["conv_w_v"].rearrange("(k p) j -> p k j", p=128)),
         inc=s_d, inc_k=16)
    P.op("sync", lambda e: e.dma_start(
         out=FS[:, 0:40].rearrange("p (h k j) -> p h k j", h=4, k=2), in_=W["fir_short_w"].rearrange("h (k p) j -> p h k j", p=128)),
         inc=s_d, inc_k=16)
    P.op("sync", lambda e: e.dma_start(
         out=FL[:].rearrange("p (h k j) -> p h k j", h=4, k=2), in_=W["fir_long_w"].rearrange("h (k p) j -> p h k j", p=128)),
         inc=s_d, inc_k=16)
    rmsrow = es.enter_context(nc.sbuf_tensor("rmsrA", [1, 256], F32))
    P.op("sync", lambda e: e.dma_start(out=rmsrow[:],
         in_=W["rms_w"].rearrange("(a x) -> a x", a=1)), inc=s_d, inc_k=16)
    ltrow = es.enter_context(nc.sbuf_tensor("ltA", [1, 8], F32))
    P.op("sync", lambda e: e.dma_start(out=ltrow[0:1, 0:1],
         in_=W["logit_temp"].rearrange("(a x) -> a x", a=1)), inc=s_d, inc_k=16)
    P.op("sync", lambda e: e.dma_start(out=ltrow[0:1, 4:8],
         in_=W["conv_res_logit"].rearrange("(a x) -> a x", a=1)), inc=s_d, inc_k=16)
    b2row = es.enter_context(nc.sbuf_tensor("b2rA", [1, 4], F32))
    P.op("sync", lambda e: e.dma_start(out=b2row[:],
         in_=W["mlp_b2"].rearrange("(a x) -> a x", a=1)), inc=s_d, inc_k=16)
    load_vec_cols(P, W["mlp_b1"], D, B1C[:], B.ident_f32, B.pm[5], s_d, B.s_tp, B.s_ev,
                  dma_waits=list(prev_done) + [B.consts_ready])
    P.fence("sync", s_d)
    P.fence("gpsimd", s_g)
    a0_dma = (s_d, s_d.n)
    a0_g = (s_g, s_g.n)
    P.op("tensor", lambda e: nc.tensor.matmul(
         B.pm[5][:, 0:256], B.ones_row_f32, rmsrow[:], start=True, stop=True),
         waits=[a0_dma, B.consts_ready, (B.s_ev, B.s_ev.n)], inc=s_t)
    P.op("vector", lambda e: e.tensor_copy(RW[:], B.pm[5][:, 0:256]),
         waits=[(s_t, s_t.n)], inc=s_v)
    lt_w = a0_dma
    def rstepA(fn, eng="vector", waits=()):
        if eng == "vector":
            P.op("vector", fn, waits=list(waits) + [(s_v, s_v.n), (s_s, s_s.n)], inc=s_v)
        else:
            P.op("scalar", fn, waits=list(waits) + [(s_v, s_v.n), (s_s, s_s.n)], inc=s_s)
    rstepA(lambda e: e.activation(ltrow[0:1, 1:2], ltrow[0:1, 0:1], AF.Abs),
           eng="scalar", waits=[lt_w, B.consts_ready])
    rstepA(lambda e: e.activation(ltrow[0:1, 1:2], ltrow[0:1, 1:2], AF.Exp, scale=-1.0),
           eng="scalar")
    rstepA(lambda e: e.activation(ltrow[0:1, 1:2], ltrow[0:1, 1:2], AF.Ln,
                                  bias=B.c_f32[0:1, 261:262]), eng="scalar")
    rstepA(lambda e: e.tensor_scalar_max(ltrow[0:1, 2:3], ltrow[0:1, 0:1], 0.0))
    rstepA(lambda e: e.tensor_add(ltrow[0:1, 2:3], ltrow[0:1, 2:3], ltrow[0:1, 1:2]))
    rstepA(lambda e: e.tensor_scalar_add(ltrow[0:1, 2:3], ltrow[0:1, 2:3], 1e-6))
    rstepA(lambda e: e.reciprocal(ltrow[0:1, 3:4], ltrow[0:1, 2:3]))
    rstepA(lambda e: e.activation(ltrow[0:1, 4:8], ltrow[0:1, 4:8], AF.Sigmoid),
           eng="scalar")
    ltv_done = (s_v, s_v.n)
    P.op("tensor", lambda e: nc.tensor.matmul(
         B.pm[5][0:4, 0:1], B.ones_row_f32[0:1, 0:4], ltrow[0:1, 3:4],
         start=True, stop=True), waits=[ltv_done, (s_s, s_s.n), (s_v, s_v.n)], inc=s_t)
    P.op("vector", lambda e: e.tensor_copy(SM[0:4, 4:5], B.pm[5][0:4, 0:1]),
         waits=[(s_t, s_t.n)], inc=s_v)
    for h in range(4):
        P.op("tensor", lambda e, h=h: nc.tensor.matmul(
             B.pm[5][:, 1 + h:2 + h], B.ones_row_f32, ltrow[0:1, 4 + h:5 + h],
             start=True, stop=True), waits=[ltv_done, (s_s, s_s.n), (s_v, s_v.n)], inc=s_t)
        P.op("vector", lambda e, h=h: e.tensor_scalar_add(
             SM[:, h:h + 1], B.pm[5][:, 1 + h:2 + h], 0.08),
             waits=[(s_t, s_t.n)], inc=s_v)
    P.op("tensor", lambda e: e.transpose(B.pm[5][0:4, 0:1], b2row[:],
                                         B.ident_f32[0:1, 0:1]),
         waits=[a0_dma, (s_v, s_v.n)], inc=s_t)
    P.op("vector", lambda e: e.tensor_copy(SM[0:4, 5:6], B.pm[5][0:4, 0:1]),
         waits=[(s_t, s_t.n)], inc=s_v)
    smalls_done = (s_v, s_v.n)
    P.op("vector", lambda e: e.memset(PRE[:, 0:PAD], 0.0), inc=s_v)
    P.op("vector", lambda e: e.memset(VS[:, 0:PAD], 0.0), inc=s_v)
    P.op("vector", lambda e: e.memset(VS[:, 2112:2112 + PAD], 0.0), inc=s_v)
    pads_done = (s_v, s_v.n)

    for tc in range(NT):
        pmx = B.pm[tc % 2]
        for k in range(ND):
            P.op("tensor", lambda e, k=k, tc=tc, pmx=pmx: nc.tensor.matmul(
                 pmx[0:4, :], WBT[:, k * 4:(k + 1) * 4],
                 B.XB[k][:, tc * 512:(tc + 1) * 512],
                 start=(k == 0), stop=(k == ND - 1)),
                 waits=list(prev_done) + [(s_g, s_g.n), (s_s, s_s.n - 1)], inc=s_t)
        P.op("scalar", lambda e, tc=tc, pmx=pmx: e.activation(
             BETA[:, tc * 512:(tc + 1) * 512], pmx[0:4, :], AF.Sigmoid),
             waits=[(s_t, s_t.n)], inc=s_s)
    beta_done = (s_s, s_s.n)
    for tt in range(NCH):
        P.op("tensor", lambda e, tt=tt: e.transpose(
             B.pt[0][0:128, 0:4], BETA[0:4, tt * 128:(tt + 1) * 128],
             B.ident_bf[0:4, 0:4]),
             waits=[beta_done, (s_v, s_v.n)], inc=B.s_tp)
        P.op("vector", lambda e, tt=tt: e.tensor_copy(
             BTM[:, tt * 4:(tt + 1) * 4], B.pt[0][0:128, 0:4]),
             waits=[(B.s_tp, B.s_tp.n)], inc=s_v)
    btm_done = (s_v, s_v.n)

    _stage_all = stageA
    for _hi, h in enumerate(head_list):
        stageA = _stage_all if _hi == 0 else min(_stage_all, stage_late)
        if _hi > 0:
            P.op("vector", lambda e: e.memset(PRE[:, 0:PAD], 0.0),
                 waits=[(s_t, s_t.n), (s_s, s_s.n)], inc=s_v)
            pads_done = (s_v, s_v.n)
        for ti, (wname, coff) in enumerate((("Wq", 0), ("Wk", 32), ("Wv", 64))):
            prev_mm = (s_t, s_t.n)
            for k in range(ND):
                P.op("gpsimd", lambda e, k=k, h=h, wname=wname: e.dma_start(
                     out=WQS[k][:],
                     in_=W[wname][k * 128:(k + 1) * 128, h * 256:(h + 1) * 256]),
                     waits=[prev_mm] + list(prev_done), inc=s_g, inc_k=16)
            P.fence("gpsimd", s_g)
            w_rdy = (s_g, s_g.n)
            for s in range(2):
                for tc in range(NT):
                    pmx = B.pm[tc % 2]
                    for k in range(ND):
                        P.op("tensor", lambda e, k=k, s=s, tc=tc, pmx=pmx: nc.tensor.matmul(
                             pmx[:], WQS[k][:, s * 128:(s + 1) * 128],
                             B.XB[k][:, tc * 512:(tc + 1) * 512],
                             start=(k == 0), stop=(k == ND - 1)),
                             waits=[w_rdy, (s_s, s_s.n - 1), pads_done] + list(prev_done),
                             inc=s_t)
                    P.op("scalar", lambda e, tc=tc, pmx=pmx: e.copy(
                         PRE[:, PAD + tc * 512:PAD + (tc + 1) * 512], pmx[:]),
                         waits=[(s_t, s_t.n), (s_v, s_v.n)], inc=s_s)
                proj_done = (s_s, s_s.n)
                g = h * 2 + s
                for j in range(4):
                    wcol = CV[:, coff + g * 4 + j:coff + g * 4 + j + 1]
                    if j == 0:
                        P.op("vector", lambda e, j=j, wcol=wcol: e.tensor_scalar_mul(
                             CACC[:], PRE[:, 60 + j:60 + j + T], wcol),
                             waits=[proj_done, (s_d, s_d.n), (s_t, s_t.n)], inc=s_v)
                    else:
                        P.op("vector", lambda e, j=j, wcol=wcol: e.scalar_tensor_tensor(
                             CACC[:], PRE[:, 60 + j:60 + j + T], wcol, CACC[:],
                             ALU.mult, ALU.add), inc=s_v)
                conv_done = (s_v, s_v.n)
                if ti == 2:
                    dst = VS[:, s * 2112 + PAD:s * 2112 + PAD + T]
                else:
                    dstT = QN if ti == 0 else KN
                    dst = dstT[:, s * T:(s + 1) * T]
                P.op("scalar", lambda e, dst=dst: e.activation(dst, CACC[:], AF.Silu),
                     waits=[conv_done], inc=s_s)
                silu_done = (s_s, s_s.n)
                if ti < 2:
                    P.op("vector", lambda e, dst=dst: e.tensor_mul(CACC[:], dst, dst),
                         waits=[silu_done], inc=s_v)
                    sq_done = (s_v, s_v.n)
                    for cc in range(NT):
                        P.op("tensor", lambda e, cc=cc, s=s: nc.tensor.matmul(
                             B.pm[2 + cc][0:1, :], B.ones_col_f32,
                             CACC[:, cc * 512:(cc + 1) * 512],
                             start=(s == 0), stop=(s == 1)),
                             waits=[sq_done, (s_v, s_v.n)], inc=s_t)
            if ti < 2:
                sums_done = (s_t, s_t.n)
                rowq = B.row4[1] if ti == 0 else B.row4[2]
                dstT = QN if ti == 0 else KN
                for cc in range(NT):
                    P.op("scalar", lambda e, cc=cc, rowq=rowq: e.activation(
                         rowq, B.pm[2 + cc][0:1, :], AF.Sqrt, bias=B.eps12_col[0:1, :]),
                         waits=[sums_done, (s_v, s_v.n)], inc=s_s)
                    P.op("vector", lambda e, rowq=rowq: e.reciprocal(rowq, rowq),
                         waits=[(s_s, s_s.n)], inc=s_v)
                    P.op("tensor", lambda e, cc=cc, rowq=rowq: nc.tensor.matmul(
                         B.pm[2 + cc][:], B.ones_row_f32, rowq,
                         start=True, stop=True),
                         waits=[(s_v, s_v.n)], inc=s_t)
                    for s in range(2):
                        P.op("vector", lambda e, cc=cc, s=s, dstT=dstT: e.tensor_mul(
                             dstT[:, s * T + cc * 512:s * T + (cc + 1) * 512],
                             dstT[:, s * T + cc * 512:s * T + (cc + 1) * 512],
                             B.pm[2 + cc][:]),
                             waits=[(s_t, s_t.n)], inc=s_v)
        qkv_done = (s_v, s_v.n)
        qkv_s = (s_s, s_s.n)
        if stageA <= 1:
            continue

        for i in range(NCH):
            for s in range(2):
                P.op("tensor", lambda e, i=i, s=s: e.transpose(
                     B.pt[0][:, 0:128], KN[:, s * T + i * 128:s * T + (i + 1) * 128],
                     B.ident_bf[:]),
                     waits=[qkv_done, qkv_s, (B.s_ev, B.s_ev.n - 1)], inc=B.s_tp)
                P.op("vector", lambda e, i=i, s=s: e.tensor_copy(
                     KTM[:, i * 256 + s * 128:i * 256 + (s + 1) * 128],
                     B.pt[0][:, 0:128]),
                     waits=[(B.s_tp, B.s_tp.n)], inc=B.s_ev)
                P.op("tensor", lambda e, i=i, s=s: e.transpose(
                     B.pt[1][:, 0:128],
                     VS[:, s * 2112 + PAD + i * 128:s * 2112 + PAD + (i + 1) * 128],
                     B.ident_bf[:]),
                     waits=[qkv_done, qkv_s, (B.s_ev, B.s_ev.n - 1)], inc=B.s_tp)
                P.op("vector", lambda e, i=i, s=s: e.tensor_copy(
                     vtm_t(i)[:, s * 128:(s + 1) * 128], B.pt[1][:, 0:128]),
                     waits=[(B.s_tp, B.s_tp.n)], inc=B.s_ev)
        tm_done = (B.s_ev, B.s_ev.n)
        if stageA <= 2:
            continue

        for i in range(NCH):
            P.op("tensor", lambda e, i=i, h=h: e.transpose(
                 B.pm[3][0:1, 0:128], BTM[:, i * 4 + h:i * 4 + h + 1], B.ident_f32[:]),
                 waits=[btm_done, (s_v, s_v.n), (s_s, s_s.n)], inc=s_t)
            P.op("vector", lambda e: e.tensor_copy(B.row4[0][0:1, 0:128],
                                                   B.pm[3][0:1, 0:128]),
                 waits=[(s_t, s_t.n)], inc=s_v)
            P.op("tensor", lambda e: nc.tensor.matmul(
                 B.pm[3][:, 0:128], B.ones_row_f32, B.row4[0][0:1, 0:128],
                 start=True, stop=True),
                 waits=[(s_v, s_v.n)], inc=s_t)
            bb_done = (s_t, s_t.n)
            for s in range(2):
                P.op("vector", lambda e, i=i, s=s: e.tensor_mul(
                     KBG[:, s * 128:(s + 1) * 128],
                     KN[:, s * T + i * 128:s * T + (i + 1) * 128], B.pm[3][:, 0:128]),
                     waits=[bb_done, tm_done], inc=s_v)
            kbg_done = (s_v, s_v.n)
            for s in range(2):
                P.op("tensor", lambda e, i=i, s=s: nc.tensor.matmul(
                     B.pm[2][:, 0:128], KN[:, s * T + i * 128:s * T + (i + 1) * 128],
                     KBG[:, s * 128:(s + 1) * 128],
                     start=(s == 0), stop=(s == 1)),
                     waits=[kbg_done, (s_v, s_v.n)], inc=s_t)
            for s in range(2):
                P.op("tensor", lambda e, i=i, s=s: nc.tensor.matmul(
                     B.pm[3][:, 0:128], KBG[:, s * 128:(s + 1) * 128],
                     KN[:, s * T + i * 128:s * T + (i + 1) * 128],
                     start=(s == 0), stop=(s == 1)),
                     waits=[kbg_done, (s_v, s_v.n)], inc=s_t)
            mm2_done = (s_t, s_t.n)
            P.op("vector", lambda e: e.tensor_mul(Mv, B.pm[2][:, 0:128], B.mask_nsu),
                 waits=[mm2_done], inc=s_v)
            P.op("vector", lambda e: e.tensor_mul(Mtv, B.pm[3][:, 0:128], B.mask_nsl),
                 inc=s_v)
            P.op("vector", lambda e: e.tensor_copy(Rv, B.ident_f32),
                 inc=s_v)
            mmt_done = (s_v, s_v.n)
            for j in range(7):
                P.op("tensor", lambda e: nc.tensor.matmul(
                     B.pm[2][:, 0:128], Mtv, Rv, start=True, stop=True),
                     waits=[mmt_done, (s_v, s_v.n)], inc=s_t)
                P.op("vector", lambda e: e.tensor_add(Rv, Rv, B.pm[2][:, 0:128]),
                     waits=[(s_t, s_t.n)], inc=s_v)
                if j < 6:
                    P.op("tensor", lambda e: nc.tensor.matmul(
                         B.pm[2][:, 0:128], Mtv, Mv, start=True, stop=True),
                         waits=[(s_v, s_v.n)], inc=s_t)
                    P.op("tensor", lambda e: nc.tensor.matmul(
                         B.pm[3][:, 0:128], Mv, Mtv, start=True, stop=True),
                         inc=s_t)
                    P.op("vector", lambda e: e.tensor_copy(Mv, B.pm[2][:, 0:128]),
                         waits=[(s_t, s_t.n)], inc=s_v)
                    P.op("vector", lambda e: e.tensor_copy(Mtv, B.pm[3][:, 0:128]),
                         inc=s_v)
                mmt_done = (s_v, s_v.n)
            P.op("vector", lambda e: e.tensor_copy(INVT[:], Rv), inc=s_v)
            invt_done = (s_v, s_v.n)
            P.op("vector", lambda e, i=i, h=h: e.tensor_scalar_mul(
                 KBT[:], vtm_t(i), BTM[:, i * 4 + h:i * 4 + h + 1]),
                 waits=[tm_done, btm_done], inc=s_v)
            vb_done = (s_v, s_v.n)
            P.op("tensor", lambda e: nc.tensor.matmul(
                 B.pm[2][:, 0:256], INVT[:], KBT[:], start=True, stop=True),
                 waits=[invt_done, vb_done], inc=s_t)
            P.op("scalar", lambda e, i=i: e.copy(
                 UTM[:, i * 256:(i + 1) * 256], B.pm[2][:, 0:256]),
                 waits=[(s_t, s_t.n)], inc=s_s)
            P.op("vector", lambda e, i=i, h=h: e.tensor_scalar_mul(
                 KBT[:], KTM[:, i * 256:(i + 1) * 256], BTM[:, i * 4 + h:i * 4 + h + 1]),
                 waits=[(s_t, s_t.n), tm_done], inc=s_v)
            kbt_done = (s_v, s_v.n)
            for s in range(2):
                P.op("tensor", lambda e, s=s: nc.tensor.matmul(
                     B.pm[3][:, 0:128], KBT[:, s * 128:(s + 1) * 128], INVT[:],
                     start=True, stop=True),
                     waits=[kbt_done, (s_s, s_s.n), (s_v, s_v.n)], inc=s_t)
                P.op("scalar", lambda e, i=i, s=s: e.copy(
                     wtf_s(s, i), B.pm[3][:, 0:128]),
                     waits=[(s_t, s_t.n)], inc=s_s)
        prep_done = (s_s, s_s.n)
        if stageA <= 3:
            continue

        for i in range(NCH):
            if i > 0 and (scanp & 1):
                for s in range(2):
                    P.op("tensor", lambda e, i=i, s=s: nc.tensor.matmul(
                         B.pm[5][:, 0:256], wtf_s(s, i),
                         SSB[:, s * 256:(s + 1) * 256],
                         start=(s == 0), stop=(s == 1)),
                         waits=[prep_done, (s_s, s_s.n), (s_v, s_v.n)], inc=s_t)
                P.op("vector", lambda e, i=i: e.tensor_sub(
                     UIB[:], UTM[:, i * 256:(i + 1) * 256], B.pm[5][:, 0:256]),
                     waits=[(s_t, s_t.n)], inc=s_v)
            else:
                P.op("vector", lambda e: e.tensor_copy(UIB[:], UTM[:, 0:256]),
                     waits=[prep_done], inc=s_v)
            uib_done = (s_v, s_v.n)
            for s in range(2) if (scanp & 2) else []:
                P.op("tensor", lambda e, i=i, s=s: nc.tensor.matmul(
                     B.pm[3][:, 0:128], KN[:, s * T + i * 128:s * T + (i + 1) * 128],
                     QN[:, s * T + i * 128:s * T + (i + 1) * 128],
                     start=(s == 0), stop=(s == 1)),
                     waits=[(s_v, s_v.n)], inc=s_t)
            if scanp & 2:
                P.op("vector", lambda e: e.tensor_mul(ATT[:], B.pm[3][:, 0:128], B.mask_tiu),
                     waits=[(s_t, s_t.n)], inc=s_v)
            att_done = (s_v, s_v.n)
            if not (scanp & 4):
                continue
            P.op("tensor", lambda e, i=i: nc.tensor.matmul(
                 B.pm[2][:, 256:512], ATT[:], UIB[:],
                 start=True, stop=True),
                 waits=[att_done, uib_done, (s_s, s_s.n), (s_v, s_v.n)], inc=s_t)
            if i > 0:
                for s in range(2):
                    P.op("tensor", lambda e, i=i, s=s: nc.tensor.matmul(
                         B.pm[2][:, 0:256], QN[:, s * T + i * 128:s * T + (i + 1) * 128],
                         SSB[:, s * 256:(s + 1) * 256],
                         start=(s == 0), stop=(s == 1)),
                         waits=[(s_s, s_s.n)], inc=s_t)
            o_done = (s_t, s_t.n)
            if i > 0:
                P.op("scalar", lambda e, i=i: e.copy(do_tile(i), B.pm[2][:, 0:256]),
                     waits=[o_done, (s_v, s_v.n)], inc=s_s)
                P.op("vector", lambda e, i=i: e.scalar_tensor_tensor(
                     do_tile(i), B.pm[2][:, 256:512], 1.0, do_tile(i),
                     ALU.mult, ALU.add),
                     waits=[(s_s, s_s.n)], inc=s_v)
            else:
                P.op("scalar", lambda e, i=i: e.copy(do_tile(i), B.pm[2][:, 256:512]),
                     waits=[o_done, (s_v, s_v.n)], inc=s_s)
            if not (scanp & 8):
                continue
            for s in range(2):
                P.op("tensor", lambda e, i=i, s=s: nc.tensor.matmul(
                     B.pm[4][:, s * 256:(s + 1) * 256],
                     KTM[:, i * 256 + s * 128:i * 256 + (s + 1) * 128], UIB[:],
                     start=(s == 0), stop=(s == 1)),
                     waits=[uib_done, (s_s, s_s.n), (s_v, s_v.n)], inc=s_t)
            if i == 0:
                P.op("vector", lambda e: e.tensor_copy(SSB32[:], B.pm[4][:]),
                     waits=[(s_t, s_t.n)], inc=s_v)
            else:
                P.op("vector", lambda e: e.tensor_add(SSB32[:], SSB32[:], B.pm[4][:]),
                     waits=[(s_t, s_t.n)], inc=s_v)
            P.op("scalar", lambda e: e.copy(SSB[:], SSB32[:]),
                 waits=[(s_v, s_v.n)], inc=s_s)
        scan_done = (s_s, s_s.n)
        if stageA <= 4:
            continue

        for (KTAP, WCOL, DSTT) in ((5, FS, LS), (64, FL, LL)):
            for s in range(2):
                g = h * 2 + s
                base = s * 2112 + PAD - (KTAP - 1)
                for j in range(KTAP):
                    wcol = WCOL[:, g * KTAP + j:g * KTAP + j + 1]
                    if j == 0:
                        P.op("vector", lambda e, wcol=wcol, base=base, j=j:
                             e.tensor_scalar_mul(CACC[:], VS[:, base + j:base + j + T], wcol),
                             waits=[scan_done, (s_d, s_d.n), (s_t, s_t.n)], inc=s_v)
                    else:
                        P.op("vector", lambda e, wcol=wcol, base=base, j=j:
                             e.scalar_tensor_tensor(CACC[:], VS[:, base + j:base + j + T],
                                                    wcol, CACC[:], ALU.mult, ALU.add),
                             inc=s_v)
                P.op("vector", lambda e: e.tensor_copy(PRE[:, 0:T], CACC[:]), inc=s_v)
                fir_done = (s_v, s_v.n)
                for i in range(NCH):
                    pb = B.pt[i % 2]
                    P.op("tensor", lambda e, i=i, pb=pb: e.transpose(
                         pb[:, 0:128], PRE[:, i * 128:(i + 1) * 128], B.ident_bf[:]),
                         waits=[fir_done, (B.s_ev, B.s_ev.n - 1)], inc=B.s_tp)
                    P.op("vector", lambda e, i=i, s=s, pb=pb, DSTT=DSTT: e.tensor_copy(
                         DSTT[:, i * 256 + s * 128:i * 256 + (s + 1) * 128], pb[:, 0:128]),
                         waits=[(B.s_tp, B.s_tp.n)], inc=B.s_ev)
        fir_all = (B.s_ev, B.s_ev.n)
        if stageA <= 5:
            continue

        for i in range(NCH):
            for si in range(4):
                if si == 2:
                    sl = do_tile(i)
                elif si == 3:
                    sl = vtm_t(i)
                else:
                    sl = (LS if si == 0 else LL)[:, i * 256:(i + 1) * 256]
                c0 = si * 4
                P.op("vector", lambda e, sl=sl, c0=c0: e.reduce_sum(
                     STT[:, c0:c0 + 1], sl, AX.X),
                     waits=[fir_all, scan_done, (s_s, s_s.n)], inc=s_v)
                P.op("vector", lambda e, sl=sl, c0=c0: e.reduce_sum(
                     STT[:, c0 + 2:c0 + 3], sl, AX.X, apply_absolute_value=True), inc=s_v)
                P.op("vector", lambda e, sl=sl: e.tensor_mul(SCS[:], sl, sl), inc=s_v)
                P.op("vector", lambda e, c0=c0: e.reduce_sum(
                     STT[:, c0 + 1:c0 + 2], SCS[:], AX.X),
                     waits=[(s_v, s_v.n)], inc=s_v)
            for si in range(4):
                c0 = si * 4
                P.op("vector", lambda e, c0=c0: e.tensor_scalar_mul(
                     STT[:, c0:c0 + 1], STT[:, c0:c0 + 1], 1.0 / 256),
                     waits=[(s_v, s_v.n)], inc=s_v)
                P.op("vector", lambda e, c0=c0: e.tensor_scalar_mul(
                     STT[:, c0 + 2:c0 + 3], STT[:, c0 + 2:c0 + 3], 1.0 / 256), inc=s_v)
                P.op("scalar", lambda e, c0=c0: e.activation(
                     SM[:, 8:9], STT[:, c0 + 1:c0 + 2], AF.Sqrt),
                     waits=[(s_v, s_v.n)], inc=s_s)
                P.op("vector", lambda e, c0=c0: e.tensor_mul(
                     SM[:, 9:10], STT[:, c0:c0 + 1], STT[:, c0:c0 + 1]),
                     waits=[(s_s, s_s.n), (s_v, s_v.n)], inc=s_v)
                P.op("vector", lambda e, c0=c0: e.scalar_tensor_tensor(
                     STT[:, c0 + 1:c0 + 2], STT[:, c0 + 1:c0 + 2], 1.0 / 256,
                     SM[:, 9:10], ALU.mult, ALU.subtract),
                     waits=[(s_v, s_v.n)], inc=s_v)
                P.op("vector", lambda e, c0=c0: e.tensor_copy(
                     STT[:, c0 + 3:c0 + 4], SM[:, 8:9]), inc=s_v)
            st_done = (s_v, s_v.n)
            P.op("tensor", lambda e, i=i: e.transpose(
                 B.pm[3][0:16, 0:128], STT[:, 0:16], B.ident_f32[:]),
                 waits=[st_done, (s_v, s_v.n)], inc=s_t)
            P.op("vector", lambda e, i=i: e.tensor_copy(
                 STF[0:16, i * 128:(i + 1) * 128], B.pm[3][0:16, 0:128]),
                 waits=[(s_t, s_t.n)], inc=s_v)
        stf_done = (s_v, s_v.n)
        if stageA <= 6:
            continue

        for hid in range(ND):
            w1s = W1S[hid % 2]
            prev_use = (s_t, s_t.n)
            for k in range(ND):
                P.op("gpsimd", lambda e, k=k, hid=hid, w1s=w1s: e.dma_start(
                     out=w1s[:, k * 128:(k + 1) * 128],
                     in_=W["mlp_w1"][k * 128:(k + 1) * 128, hid * 128:(hid + 1) * 128]),
                     waits=[prev_use], inc=s_g, inc_k=16)
            P.fence("gpsimd", s_g)
            w1_rdy = (s_g, s_g.n)
            for tc in range(NT):
                pmx = B.pm[(hid * NT + tc) % 2]
                hg = PRE[:, (tc % 2) * 512:(tc % 2) * 512 + 512]
                for k in range(ND):
                    P.op("tensor", lambda e, k=k, tc=tc, pmx=pmx, w1s=w1s:
                         nc.tensor.matmul(pmx[:], w1s[:, k * 128:(k + 1) * 128],
                                          B.XB[k][:, tc * 512:(tc + 1) * 512],
                                          start=(k == 0), stop=False),
                         waits=[stf_done, w1_rdy, (s_s, s_s.n - 1)] + list(prev_done),
                         inc=s_t)
                P.op("tensor", lambda e, hid=hid, tc=tc, pmx=pmx: nc.tensor.matmul(
                     pmx[:], W1B[0:16, hid * 128:(hid + 1) * 128],
                     STF[0:16, tc * 512:(tc + 1) * 512],
                     start=False, stop=True),
                     waits=[stf_done], inc=s_t)
                P.op("scalar", lambda e, hid=hid, pmx=pmx, hg=hg: e.activation(
                     hg, pmx[:], AF.Gelu, bias=B1C[:, hid:hid + 1]),
                     waits=[(s_t, s_t.n), (s_v, s_v.n)], inc=s_s)
                if not (gp & 1):
                    continue
                P.op("tensor", lambda e, hid=hid, tc=tc, hg=hg: nc.tensor.matmul(
                     B.pm[2 + tc % 2][0:4, :], W2T[:, hid * 4:(hid + 1) * 4], hg,
                     start=True, stop=True),
                     waits=[(s_s, s_s.n), (s_v, s_v.n - 1)], inc=s_t)
                if hid == 0:
                    P.op("vector", lambda e, tc=tc: e.tensor_copy(
                         LGA[0:4, tc * 512:(tc + 1) * 512], B.pm[2 + tc % 2][0:4, :]),
                         waits=[(s_t, s_t.n)], inc=s_v)
                else:
                    P.op("vector", lambda e, tc=tc: e.tensor_add(
                         LGA[0:4, tc * 512:(tc + 1) * 512],
                         LGA[0:4, tc * 512:(tc + 1) * 512], B.pm[2 + tc % 2][0:4, :]),
                         waits=[(s_t, s_t.n)], inc=s_v)
        w2_done = (s_v, s_v.n)
        if not (gp & 2):
            continue
        for tc in range(NT):
            P.op("vector", lambda e, tc=tc: e.tensor_scalar(
                 LGA[0:4, tc * 512:(tc + 1) * 512], LGA[0:4, tc * 512:(tc + 1) * 512],
                 SM[0:4, 5:6], SM[0:4, 4:5], ALU.add, ALU.mult),
                 waits=[w2_done, smalls_done], inc=s_v)
        lg_done = (s_v, s_v.n)
        for i in range(NCH):
            P.op("tensor", lambda e, i=i: e.transpose(
                 B.pm[2 + i % 2][0:128, 0:4], LGA[0:4, i * 128:(i + 1) * 128],
                 B.ident_f32[0:4, 0:4]),
                 waits=[lg_done, (s_v, s_v.n - 1)], inc=B.s_tp)
            P.op("vector", lambda e, i=i: e.tensor_copy(
                 PR4[:, i * 4:(i + 1) * 4], B.pm[2 + i % 2][0:128, 0:4]),
                 waits=[(B.s_tp, B.s_tp.n)], inc=s_v)
        def gstep(fn, waits=()):
            for i in range(NCH):
                P.op("vector", lambda e, i=i, fn=fn: fn(e, i), waits=list(waits), inc=s_v)
        prs4 = [PR4[:, i * 4:(i + 1) * 4] for i in range(NCH)]
        if not (gp & 4):
            continue
        gstep(lambda e, i: e.reduce_max(STT[:, i:i + 1], prs4[i], AX.X, negate=True),
              waits=[(s_v, s_v.n)])
        exw = (s_v, s_v.n)
        for i in range(NCH):
            P.op("scalar", lambda e, i=i: e.activation(prs4[i], prs4[i], AF.Exp,
                                                       bias=STT[:, i:i + 1]),
                 waits=[exw], inc=s_s)
        smw = (s_s, s_s.n)
        for i in range(NCH):
            P.op("vector", lambda e, i=i: e.reduce_sum(STT[:, i:i + 1], prs4[i], AX.X),
                 waits=[smw], inc=s_v)
        gstep(lambda e, i: e.reciprocal(STT[:, i:i + 1], STT[:, i:i + 1]))
        gstep(lambda e, i: e.tensor_scalar_mul(prs4[i], prs4[i], STT[:, i:i + 1]))
        if not (gp & 8):
            continue
        gstep(lambda e, i, h=h: e.tensor_scalar(prs4[i][:, 0:1], prs4[i][:, 0:1], 0.84,
                                                SM[:, h:h + 1], ALU.mult, ALU.add))
        gstep(lambda e, i: e.tensor_scalar(prs4[i][:, 1:2], prs4[i][:, 1:2], 0.84,
                                           0.08, ALU.mult, ALU.add))
        gstep(lambda e, i: e.tensor_scalar_mul(prs4[i][:, 2:4], prs4[i][:, 2:4], 0.84))
        pr_done = (s_v, s_v.n)
        if stageA <= 7:
            continue

        if dbg is not None and "LSE" in dbg and h == head_list[0]:
            P.op("sync", lambda e: e.dma_start(out=dbg["LSE"], in_=QN[:]),
                 waits=[pr_done, (s_s, s_s.n)], inc=s_d, inc_k=16)
        for i in range(NCH):
            ot = OT[:, i * 256:(i + 1) * 256]
            P.op("vector", lambda e, i=i, ot=ot: e.tensor_scalar_mul(
                 ot, LS[:, i * 256:(i + 1) * 256], prs4[i][:, 0:1]),
                 waits=[pr_done, (s_s, s_s.n)], inc=s_v)
            P.op("vector", lambda e, i=i, ot=ot: e.scalar_tensor_tensor(
                 ot, LL[:, i * 256:(i + 1) * 256], prs4[i][:, 1:2], ot,
                 ALU.mult, ALU.add), waits=[(s_v, s_v.n)], inc=s_v)
            P.op("vector", lambda e, i=i, ot=ot: e.scalar_tensor_tensor(
                 ot, do_tile(i), prs4[i][:, 2:3], ot, ALU.mult, ALU.add),
                 waits=[(s_v, s_v.n)], inc=s_v)
            P.op("vector", lambda e, i=i, ot=ot: e.scalar_tensor_tensor(
                 ot, vtm_t(i), prs4[i][:, 3:4], ot, ALU.mult, ALU.add),
                 waits=[(s_v, s_v.n)], inc=s_v)
            P.op("vector", lambda e, i=i, ot=ot: e.tensor_mul(SCS[:], ot, ot),
                 waits=[(s_v, s_v.n)], inc=s_v)
            P.op("vector", lambda e, i=i: e.reduce_sum(STT[:, 0:1], SCS[:], AX.X),
                 waits=[(s_v, s_v.n)], inc=s_v)
            P.op("scalar", lambda e, i=i: e.activation(
                 STT[:, 1:2], STT[:, 0:1], AF.Sqrt, scale=1.0 / 256, bias=B.eps5_col),
                 waits=[(s_v, s_v.n)], inc=s_s)
            P.op("vector", lambda e, i=i: e.reciprocal(STT[:, 2:3], STT[:, 1:2]),
                 waits=[(s_s, s_s.n)], inc=s_v)
            if dbg is None or "RAWO" not in dbg:
                P.op("vector", lambda e, i=i, ot=ot: e.tensor_scalar_mul(
                     ot, ot, STT[:, 2:3]), waits=[(s_v, s_v.n)], inc=s_v)
                P.op("vector", lambda e, i=i, ot=ot: e.tensor_mul(ot, ot, RW[:]),
                     waits=[(s_v, s_v.n), (s_t, s_t.n)], inc=s_v)
            for s in range(2):
                pb = B.pt[s % 2]
                P.op("tensor", lambda e, i=i, s=s, pb=pb, ot=ot: e.transpose(
                     pb[:, 0:128], ot[:, s * 128:(s + 1) * 128], B.ident_bf[:]),
                     waits=[(s_v, s_v.n), (B.s_ev, B.s_ev.n - 1)], inc=B.s_tp)
                P.op("vector", lambda e, i=i, s=s, h=h, pb=pb: e.tensor_copy(
                     B.ACC[h * 2 + s][:, i * 128:(i + 1) * 128], pb[:, 0:128]),
                     waits=[(B.s_tp, B.s_tp.n)], inc=B.s_ev)
        if dbg is not None and "ACC0E" in dbg and h == head_list[0]:
            P.op("sync", lambda e, h=h: e.dma_start(out=dbg["ACC0E"], in_=B.ACC[h * 2][:]),
                 waits=[(B.s_ev, B.s_ev.n)], inc=s_d, inc_k=16)
    stageA = _stage_all
    acc_done = (B.s_ev, B.s_ev.n)
    if acc_out is not None:
        for ck in range(4):
            P.op("sync", lambda e, ck=ck: e.dma_start(out=acc_out[ck], in_=B.ACC[ck][:]),
                 waits=[acc_done], inc=s_d, inc_k=16)
        fin = s_d.n
        P.raw("sync", lambda e, dedup, fin=fin: e.wait_ge(s_d.h, fin))
        return [(s_d, fin)]
    if acc_in is not None:
        for ck in range(4):
            P.op("sync", lambda e, ck=ck: e.dma_start(out=B.ACC[ck][:], in_=acc_in[ck]),
                 waits=list(prev_done), inc=s_d, inc_k=16)
        P.fence("sync", s_d)
        acc_done = (s_d, s_d.n)
    if stageA <= 8:
        fins = [(s_v, s_v.n), (s_s, s_s.n), (s_g, s_g.n), (s_d, s_d.n),
                (B.s_ev, B.s_ev.n)]
        P.raw("sync", lambda e, dedup, fins=fins: [e.wait_ge(s.h, v) for (s, v) in fins if v > 0])
        return fins
    if dbg is not None:
        allw = [acc_done, (s_v, s_v.n), (s_s, s_s.n), (s_t, s_t.n)]
        pairs = {"QN": QN, "KN": KN, "UTM": UTM, "KTM": KTM}
        for nm, tens in pairs.items():
            if nm in dbg:
                P.op("sync", lambda e, nm=nm, tens=tens: e.dma_start(
                     out=dbg[nm], in_=tens[:]), waits=allw, inc=s_d, inc_k=16)
        if "DOD" in dbg:
            for j in range(4):
                P.op("sync", lambda e, j=j: e.dma_start(
                     out=dbg["DOD"][:, j * 1024:(j + 1) * 1024], in_=DO[j]),
                     waits=allw, inc=s_d, inc_k=16)
        if "VTMD" in dbg:
            for j in range(4):
                P.op("sync", lambda e, j=j: e.dma_start(
                     out=dbg["VTMD"][:, j * 1024:(j + 1) * 1024], in_=RSB[j]),
                     waits=allw, inc=s_d, inc_k=16)
        if "STF" in dbg:
            P.op("sync", lambda e: e.dma_start(out=dbg["STF"], in_=STF[:]),
                 waits=allw, inc=s_d, inc_k=16)
        if "PR4" in dbg:
            P.op("sync", lambda e: e.dma_start(out=dbg["PR4"], in_=PR4[:]),
                 waits=allw, inc=s_d, inc_k=16)
        if "ACC" in dbg:
            for ck in range(ND):
                P.op("sync", lambda e, ck=ck: e.dma_start(out=dbg["ACC"][ck],
                     in_=B.ACC[ck][:]), waits=allw, inc=s_d, inc_k=16)

    wo = [QN[:, k * 1024:(k + 1) * 1024] for k in range(4)] + \
         [KN[:, k * 1024:(k + 1) * 1024] for k in range(4)]
    for k in range(ND):
        P.op("gpsimd", lambda e, k=k: e.dma_start(
             out=wo[k], in_=W["Wo"][k * 128:(k + 1) * 128, :]),
             waits=[acc_done, (s_v, s_v.n), (s_t, s_t.n), (s_d, s_d.n)],
             inc=s_g, inc_k=16)
    P.fence("gpsimd", s_g)
    wo_rdy = (s_g, s_g.n)
    for tc in range(NT):
        for ct in range(ND):
            pmx = B.pm[ct % 2]
            for k in range(ND):
                P.op("tensor", lambda e, ct=ct, k=k, tc=tc, pmx=pmx: nc.tensor.matmul(
                     pmx[:], wo[k][:, ct * 128:(ct + 1) * 128],
                     B.ACC[k][:, tc * 512:(tc + 1) * 512],
                     start=(k == 0), stop=(k == ND - 1)),
                     waits=[wo_rdy, acc_done, (s_v, s_v.n - 1), (s_s, s_s.n)],
                     inc=s_t)
            P.op("vector", lambda e, ct=ct, tc=tc, pmx=pmx: e.scalar_tensor_tensor(
                 B.RS[ct][:], pmx[:], 1.0, B.XB[ct][:, tc * 512:(tc + 1) * 512],
                 ALU.mult, ALU.add),
                 waits=[(s_t, s_t.n), (s_v, s_v.n), (B.s_ln_ev, B.s_ln_ev.n)], inc=s_v)
        rs_ready = [(s_v, s_v.n)]
        if dbg is not None and "RS" in dbg and tc == 0:
            for ct in range(ND):
                P.op("sync", lambda e, ct=ct: e.dma_start(out=dbg["RS"][ct],
                     in_=B.RS[ct][:]), waits=rs_ready, inc=s_d, inc_k=16)
            rs_ready = rs_ready + [(s_d, s_d.n)]
        emit_ln(P, B, tc, rs_ready, B.XB)
    return [(B.s_ln_ev, B.s_ln_ev.n)]



def build_phaseA_standalone(head_list=None, dbg_names=(), stageA=99, scanp=15, gp=15, part=None, stage_late=99):
    nc = bass.Bass("TRN2", target_bir_lowering=False, detect_race_conditions=RACE_DETECT)
    x_in = nc.dram_tensor("x_in", [T, D], F32, kind="ExternalInput").ap()
    x_out = nc.dram_tensor("x_out", [T, D], F32, kind="ExternalOutput").ap()
    shapes = dict(Wq=[D, D], Wk=[D, D], Wv=[D, D], Wb=[D, 4],
                  conv_w_q=[D, 4], conv_w_k=[D, 4], conv_w_v=[D, 4],
                  fir_short_w=[4, 256, 5], fir_long_w=[4, 256, 64],
                  mlp_w1=[D + 16, D], mlp_b1=[D], mlp_w2=[D, 4], mlp_b2=[4],
                  logit_temp=[1], conv_res_logit=[4], rms_w=[256], Wo=[D, D])
    W = {n: nc.dram_tensor(n, sh, F32, kind="ExternalInput").ap() for n, sh in shapes.items()}
    cf = nc.dram_tensor("c_f32h", [128, CW], F32, kind="ExternalInput").ap()
    cb = nc.dram_tensor("c_bf16h", [128, CWB], BF16, kind="ExternalInput").ap()
    accio = None
    if part == 1:
        accio = [nc.dram_tensor(f"acc_h{i}", [128, T], BF16, kind="ExternalOutput").ap()
                 for i in range(4)]
        head_list = [0, 1]
    elif part == 2:
        accio = [nc.dram_tensor(f"acc_h{i}", [128, T], BF16, kind="ExternalInput").ap()
                 for i in range(4)]
        head_list = [2, 3]
    dbg = {}
    for nm in dbg_names:
        if nm == "RS":
            dbg[nm] = [nc.dram_tensor(f"d_RS{i}", [128, 512], F32, kind="ExternalOutput").ap()
                       for i in range(ND)]
        elif nm in ("QN", "KN", "UTM", "WTF", "VTM", "KTM", "LSD", "LLD", "DOD", "OTD"):
            dbg[nm] = nc.dram_tensor("d_" + nm, [128, 2 * T], BF16, kind="ExternalOutput").ap()
        elif nm == "RAWO":
            dbg[nm] = True
        elif nm == "LSE":
            dbg[nm] = nc.dram_tensor("d_LSE", [128, 2 * T], BF16, kind="ExternalOutput").ap()
        elif nm == "VTMD":
            dbg[nm] = nc.dram_tensor("d_VTMD", [128, 2 * T], BF16, kind="ExternalOutput").ap()
        elif nm == "STF":
            dbg[nm] = nc.dram_tensor("d_STF", [16, T], BF16, kind="ExternalOutput").ap()
        elif nm == "PR4":
            dbg[nm] = nc.dram_tensor("d_PR4", [128, 64], F32, kind="ExternalOutput").ap()
        elif nm == "ACC0E":
            dbg[nm] = nc.dram_tensor("d_ACC0E", [128, T], BF16, kind="ExternalOutput").ap()
        elif nm == "ACC":
            dbg[nm] = [nc.dram_tensor(f"d_ACC{i}", [128, T], BF16, kind="ExternalOutput").ap()
                       for i in range(ND)]
    es = ExitStack()
    P = Prog(nc, es)
    with es:
        B = alloc_common(P, nc)
        load_consts(P, B, cf, cb)
        load_x_to_fm(P, x_in, B.XB, B.ident_bf, [p[:, 0:128] for p in B.pt],
                     B.s_g, B.s_tp, B.s_ev, extra=[B.consts_ready])
        x_ready = [(B.s_ev, B.s_ev.n), B.consts_ready]
        done = emit_phaseA(P, B, W, x_ready, head_list=head_list,
                           dbg=(dbg if dbg else None), stageA=stageA, scanp=scanp, gp=gp,
                           stage_late=stage_late,
                           acc_out=(accio if part == 1 else None),
                           acc_in=(accio if part == 2 else None))
        if stageA > 8 and part != 1:
            emit_store_xb_as_f32(P, B, x_out, done)
        with nc.Block() as block:
            P.finalize(block)
    nc._P = P
    return nc



def kernel(**inputs):
    xs = np.ascontiguousarray(np.asarray(inputs["L0_attn_hidden_states"], dtype=np.float32))
    cf = make_const_f32()
    cb = make_const_bf16()
    cores = list(range(NCORES))

    def amap(i, x):
        m = dict(x_in=x[i], c_f32h=cf, c_bf16h=cb)
        for n in ("Wq", "Wk", "Wv", "Wb", "conv_w_q", "conv_w_k", "conv_w_v",
                  "fir_short_w", "fir_long_w", "mlp_w1", "mlp_b1", "mlp_w2",
                  "mlp_b2", "logit_temp", "conv_res_logit", "rms_w", "Wo"):
            m[n] = np.asarray(inputs["L0_attn_" + n], dtype=np.float32)
        return m

    ncA1 = build_phaseA_standalone(part=1)
    resA1 = run_bass_kernel_spmd(ncA1, [amap(i, xs) for i in cores], core_ids=cores)
    ncA2 = build_phaseA_standalone(part=2)
    maps2 = []
    for i in cores:
        m = amap(i, xs)
        for ck in range(4):
            m[f"acc_h{ck}"] = np.asarray(resA1.results[i][f"acc_h{ck}"])
        maps2.append(m)
    resA2 = run_bass_kernel_spmd(ncA2, maps2, core_ids=cores)
    x1 = np.stack([np.asarray(resA2.results[i]["x_out"]) for i in cores])

    def bmap(i, x):
        m = dict(x_in=x[i], c_f32h=cf, c_bf16h=cb)
        for n in ("ln_g", "ln_b", "bn_g", "bn_b", "bn_mean", "bn_var", "Wr", "br",
                  "W_down", "b_down", "W_up", "b_up"):
            m[n] = np.asarray(inputs["L0_ffn_" + n], dtype=np.float32)
        return m

    ncB = build_phaseB_standalone()
    resB = run_bass_kernel_spmd(ncB, [bmap(i, x1) for i in cores], core_ids=cores)
    x2 = np.stack([np.asarray(resB.results[i]["x_out"]) for i in cores])

    def cmap(i, x):
        m = dict(x_in=x[i], c_f32h=cf, c_bf16h=cb)
        for n in ("Wq", "Wk", "Wv", "Wout", "bout"):
            m[n] = np.asarray(inputs["L1_attn_" + n], dtype=np.float32)
        return m

    ncC = build_phaseC_standalone()
    resC = run_bass_kernel_spmd(ncC, [cmap(i, x2) for i in cores], core_ids=cores)
    x3 = np.stack([np.asarray(resC.results[i]["x_out"]) for i in cores])

    def dmap(i, x):
        m = dict(x_in=x[i], c_f32h=cf, c_bf16h=cb)
        for n in ("Wg", "bg", "W1", "b1", "W2", "b2"):
            m[n] = np.asarray(inputs["L1_ffn_" + n], dtype=np.float32)
        return m

    ncD = build_phaseD_standalone()
    resD = run_bass_kernel_spmd(ncD, [dmap(i, x3) for i in cores], core_ids=cores)
    out = np.stack([np.asarray(resD.results[i]["x_out"]) for i in cores])
    return out.astype(np.float32)
